# version 16
# baseline (speedup 1.0000x reference)
import numpy as np
import ml_dtypes

import concourse.bass as bass
import concourse.mybir as mybir
from concourse.bass_utils import run_bass_kernel_spmd
from concourse.aot_env import allow_ici_all_to_all

BFNP = ml_dtypes.bfloat16
F32 = mybir.dt.float32
BF16 = mybir.dt.bfloat16
Alu = mybir.AluOpType
Act = mybir.ActivationFunctionType
AXX = mybir.AxisListType.X

NCORE = 8
D = 1024
T = 16384
NT = 2 * T
RT = NT // NCORE
CK = 1024
NCH = T // CK
HID = 4096
SC = 256
NSC = RT // SC
MAGIC = float(np.float32(1.5 * 2 ** 23))
_TAP = {"name": None}


def build_program():
    nc = bass.Bass("TRN2", target_bir_lowering=False, debug=False)

    def din(n, s):
        return nc.dram_tensor(n, s, F32, kind="ExternalInput").ap()

    x_tok = din("x_tok", [RT, D])
    x_feat = din("x_feat", [128, NT])
    x_ftok = din("x_ftok", [D, RT])
    x_edge = din("x_edge", [D, 2])
    a_blk = din("a_blk", [128, 128])
    b_blk = din("b_blk", [128, 128])
    c_blk = din("c_blk", [128, 128])
    skip2 = din("skip2", [128, 1])
    nw_sl = din("nw_sl", [128, 1])
    w_outT = din("w_outT", [D, D])
    b_out_c = din("b_out_c", [D, 1])
    w1T_in = din("w1T_in", [D, HID])
    w2T_in = din("w2T_in", [HID, D])
    cw_in = din("cw_in", [HID, 3])
    cb_in = din("cb_in", [HID, 1])
    ident_in = din("ident_in", [128, 128])
    onesc_in = din("onesc_in", [128, 1])
    hmask_in = din("hmask_in", [128, 2])
    mgc_in = din("mgc_in", [128, 2])
    y_out = nc.dram_tensor("y_out", [D, RT], F32, kind="ExternalOutput").ap()

    ag_st_in = nc.dram_tensor("ag_st_in", [3, 32, 128], F32).ap()
    ag_st_out = nc.dram_tensor("ag_st_out", [NCORE, 3, 32, 128], F32).ap()
    a2a_in = nc.dram_tensor("a2a_in", [NCORE, 128, RT], BF16).ap()
    a2a_out = nc.dram_tensor("a2a_out", [NCORE, 128, RT], BF16).ap()
    egA_in = nc.dram_tensor("egA_in", [D, 2], BF16).ap()
    egA_out = nc.dram_tensor("egA_out", [NCORE, D, 2], BF16).ap()
    egB_in = nc.dram_tensor("egB_in", [1, 2], F32).ap()
    egB_out = nc.dram_tensor("egB_out", [NCORE, 1, 2], F32).ap()
    w2q_dram = nc.dram_tensor("w2q_dram", [NCORE, 128, HID], BF16).ap()
    w1q_dram = nc.dram_tensor("w1q_dram", [32, 128, D], BF16).ap()
    rowb = nc.dram_tensor("rowb", [10, 520], F32).ap()

    keep = []

    def sbuf(name, shape, dtype):
        g = nc.sbuf_tensor(name, shape, dtype)
        t = g.__enter__()
        keep.append(g)
        return t

    ARENA = 156 * 1024
    arena_guard = nc.sbuf_tensor("arena", [128, ARENA], mybir.dt.uint8)
    arena_t = arena_guard.__enter__()
    keep.append(arena_guard)
    arena_base = nc.lookup_mloc(arena_t).addr
    arena_off = {"S": 0, "F": 0}

    def asb(grp, name, shape, dtype):
        per_part = int(np.prod(shape[1:])) * mybir.dt.size(dtype)
        off = arena_off[grp]
        arena_off[grp] = (off + per_part + 31) // 32 * 32
        assert arena_off[grp] <= ARENA, f"arena {grp} overflow at {name}"
        h = nc.virtual_sbuf_tensor(name, list(shape), dtype)
        nc._tensor(h.name, h.shape, h.dtype, type="SB")
        mloc = nc.lookup_mloc(h)
        mloc.allocated = True
        mloc.addr = arena_base + off
        return h

    def psum_(name, shape):
        g = nc.psum_tensor(name, list(shape), F32)
        t = g.__enter__()
        keep.append(g)
        return t

    def mksem(name):
        g = nc.semaphore(name)
        s = g.__enter__()
        keep.append(g)
        return s

    STREAM = {e: [] for e in ("tensor", "scalar", "vector", "gpsimd", "sync")}
    CNT = {}

    class Sem:
        def __init__(self, name):
            self.h = mksem(name)
            self.name = name
            CNT[name] = 0

        def bump(self, by):
            CNT[self.name] += by
            return CNT[self.name]

        @property
        def v(self):
            return CNT[self.name]

    def emit(eng, fn, wait=None, inc=None, inc_by=1, fuse=False):
        waits = [] if wait is None else (wait if isinstance(wait, list) else [wait])
        waits = [(s, v) for (s, v) in waits if v is not None and v > 0]
        val = inc.bump(inc_by) if inc is not None else None
        ih = inc.h if inc is not None else None

        def build(h):
            if fn is None:
                for (s, v) in waits:
                    h.wait_ge(s.h, v)
                return
            if fuse and len(waits) == 1:
                i = fn(h)
                i._wait_ge(waits[0][0].h, waits[0][1])
            else:
                for (s, v) in waits:
                    h.wait_ge(s.h, v)
                i = fn(h)
            if ih is not None:
                if eng == "vector":
                    h.drain().then_inc(ih, inc_by)
                else:
                    i.then_inc(ih, inc_by)
        STREAM[eng].append(build)
        return val

    def wait(eng, sem, val):
        emit(eng, None, wait=(sem, val))

    def dma(eng, sem, out, in_, wait=None, slow=False):
        if slow:
            return emit(eng, lambda h: h.dma_start(out=out, in_=in_,
                                                   allow_slow_non_contiguous=True),
                        wait=wait, inc=sem, inc_by=16)
        return emit(eng, lambda h: h.dma_start(out=out, in_=in_), wait=wait,
                    inc=sem, inc_by=16)

    rowb_last = [0] * 10

    def repl(sem, out, row_tensor, offset, n, wait=None, slot=None):
        th = getattr(row_tensor, "tensor", row_tensor)
        if type(th).__name__ == "DRamTensorHandle":
            srcap = bass.AP(th, offset, [[0, 128], [1, n]])
            return emit("gpsimd", lambda h: h.dma_start(out=out, in_=srcap),
                        wait=wait, inc=sem, inc_by=16)
        assert slot is not None
        w_ = [] if wait is None else (wait if isinstance(wait, list) else [wait])
        if rowb_last[slot]:
            w_.append((sem, rowb_last[slot]))
        assert offset == 0
        row_ap = row_tensor[0:1, 0:n]
        v1 = emit("gpsimd", lambda h: h.dma_start(out=rowb[slot:slot + 1, 0:n],
                                                  in_=row_ap),
                  wait=w_ or None, inc=sBW, inc_by=16)
        srcap = bass.AP(rowb.tensor, slot * 520, [[0, 128], [1, n]])
        v2 = emit("gpsimd", lambda h: h.dma_start(out=out, in_=srcap),
                  wait=[(sBW, v1)], inc=sem, inc_by=16)
        rowb_last[slot] = v2
        return v2

    dbg_state = {"n": 0}
    snap_bufs = []

    def snap_alloc(shape, dtype):
        t = sbuf(f"snapb{len(snap_bufs)}", list(shape), dtype)
        snap_bufs.append((t, shape, dtype))
        return t

    SNAP = _TAP["name"] == "snap"
    snaps = {}
    if SNAP:
        snaps = {
            "w1s": snap_alloc([128, D], BF16),
            "xqp": snap_alloc([128, SC + 2], BF16),
            "h0": snap_alloc([128, SC + 2], F32),
            "g0": snap_alloc([128, SC], F32),
            "xq20": snap_alloc([128, SC], BF16),
            "s3r": snap_alloc([128, SC], F32),
            "r3r": snap_alloc([128, SC], F32),
            "y0": snap_alloc([128, SC], F32),
        }

    def snapcpy(name, srcap, wait=None):
        if not SNAP:
            return
        emit("vector", lambda v: v.tensor_copy(snaps[name][:, :], srcap), wait=wait,
             inc=sV)

    def snap_dump():
        if not SNAP:
            return
        for i, nm in enumerate(snaps):
            t, shape, dtype = snap_bufs[i][0], snap_bufs[i][1], snap_bufs[i][2]
            dout = nc.dram_tensor(f"dbg{i}", list(shape), dtype,
                                  kind="ExternalOutput").ap()
            dma("gpsimd", sG, dout[:, :], t[:, :], wait=[(sY, CNT["sY"])])

    def tap(name, parts):
        if _TAP["name"] != name:
            return
        for (ap, dtype, shape, wait_) in parts:
            dn = f"dbg{dbg_state['n']}"
            dbg_state['n'] += 1
            dout = nc.dram_tensor(dn, list(shape), dtype, kind="ExternalOutput").ap()
            emit("gpsimd", lambda h, dout=dout, ap=ap: h.dma_start(out=dout[:, :],
                                                                   in_=ap),
                 wait=wait_, inc=sG, inc_by=16)

    sSET = Sem("sSET")
    sWO = [Sem("sWO0"), Sem("sWO1")]
    sXT = [Sem("sXT0"), Sem("sXT1")]
    sWQ = Sem("sWQ")
    sXF = [Sem("sXF0"), Sem("sXF1")]
    sEO = Sem("sEO")
    sOUT = Sem("sOUT")
    sXF2 = [Sem("sXF2_0"), Sem("sXF2_1")]
    sW1S = [Sem("sW1S0"), Sem("sW1S1")]
    sW2S = [Sem("sW2S0"), Sem("sW2S1")]
    sST = Sem("sST")
    sRP = Sem("sRP")
    sA2A = [Sem("sA2A0"), Sem("sA2A1")]
    sWQG = Sem("sWQG")
    sEM = Sem("sEM")
    sEGW = Sem("sEGW")
    sHL = Sem("sHL")
    sBW = Sem("sBW")
    sBC = Sem("sBC")
    sG = Sem("sG")
    sV = Sem("sV")
    sP = Sem("sP")
    sA = Sem("sA")
    sC = Sem("sC")
    sM0 = Sem("sM0")
    sM1 = Sem("sM1")
    sT0 = Sem("sT0")
    sT1 = Sem("sT1")
    sY = Sem("sY")
    sMb = {0: sM0, 1: sM1}
    sTb = {0: sT0, 1: sT1}

    pb_sc = [psum_(f"pbsc{i}", (128, 512)) for i in range(4)]
    pb = [psum_(f"pb{i}", (128, 512)) for i in range(4)]
    pbr = {0: 0, 1: 0, 2: 0, 3: 0, 4: 0, 5: 0, 6: 0, 7: 0}

    def pe_pb(x, fn, wait=None, tag=None):
        w = [] if wait is None else (wait if isinstance(wait, list) else [wait])
        if pbr[x]:
            w.append((sV, pbr[x]))
        return emit("tensor", fn, wait=w or None, inc=sP)

    def dv_pb(x, fn, wait=None):
        v = emit("vector", fn, wait=wait, inc=sV)
        pbr[x] = v
        return v

    wA = sbuf("wA", [128, 128], BF16)
    wB = sbuf("wB", [128, 128], BF16)
    wC = sbuf("wC", [128, 128], BF16)
    skip_s = sbuf("skip_s", [128, 1], F32)
    nw_s = sbuf("nw_s", [128, 1], F32)
    ident_s = sbuf("ident_s", [128, 128], F32)
    onesc_s = sbuf("onesc_s", [128, 1], F32)
    hmask_s = sbuf("hmask_s", [128, 2], F32)
    bout_s = sbuf("bout_s", [128, 8], F32)
    cw_s = sbuf("cw_s", [128, 96], F32)
    cb_s = sbuf("cb_s", [128, 32], F32)
    sw1_c = sbuf("sw1_c", [128, 1], F32)
    sw2_c = sbuf("sw2_c", [128, 1], F32)
    sw1_row = sbuf("sw1_row", [1, 1], F32)
    sw2_row = sbuf("sw2_row", [1, 1], F32)
    mgc_s = sbuf("mgc_s", [128, 2], F32)
    woutb = [sbuf(f"woutb{k}", [128, D], BF16) for k in range(8)]
    xqh_l = [sbuf(f"xqhl{k}", [128, 1], BF16) for k in range(8)]
    xqh_r = [sbuf(f"xqhr{k}", [128, 1], BF16) for k in range(8)]
    r1h = sbuf("r1h", [1, 2], F32)

    stage = asb("S", "stage", [128, 384], F32)
    xf_p = [asb("S", f"xf{p}", [128, 2 * CK], F32) for p in range(3)]
    xr_p = [asb("S", f"xr{p}", [128, 2 * CK], BF16) for p in range(3)]
    u_p = [asb("S", f"u{p}", [128, 2 * CK], F32) for p in range(2)]
    z_p = [asb("S", f"z{p}", [128, 2 * (CK + 1)], BF16) for p in range(2)]
    mx_p = [asb("S", f"mxp{p}", [128, 2 * CK], BF16) for p in range(2)]
    mbc = asb("S", "mbc", [128, CK], F32)
    sbc = asb("S", "sbc", [128, CK], F32)
    xt_p = [asb("S", f"xt{p}", [128, D], F32) for p in range(2)]
    sq_s = asb("S", "sq_s", [128, D], F32)
    st_m = asb("S", "st_m", [128, 32], F32)
    st_l = asb("S", "st_l", [128, 32], F32)
    st_s = asb("S", "st_s", [128, 32], F32)
    st_r = asb("S", "st_r", [128, 32], F32)
    st_q = asb("S", "st_q", [128, 32], F32)
    st_row = asb("S", "st_row", [32, 128], F32)
    c_t1 = asb("S", "c_t1", [128, 1], F32)
    c_t2 = asb("S", "c_t2", [128, 1], F32)
    c_t3 = asb("S", "c_t3", [128, 1], F32)
    wq_ld = [asb("S", f"wqld{p}", [128, HID], F32) for p in range(2)]
    wq_c1 = asb("S", "wq_c1", [128, 8], F32)
    wq_c2 = asb("S", "wq_c2", [128, 32], F32)
    w2tile = asb("S", "w2tile", [128, D], BF16)
    w1tile = asb("S", "w1tile", [128, HID], BF16)
    e_outs = [asb("S", f"eo{k}", [128, 2], BF16) for k in range(8)]
    e_x = [asb("S", f"ex{k}", [128, 2], F32) for k in range(8)]
    e_x1 = [asb("S", f"ex1{k}", [128, 2], F32) for k in range(8)]
    e_xq = [asb("S", f"exq{k}", [128, 2], BF16) for k in range(8)]
    e_m = asb("S", "e_m", [128, 2], F32)
    e_l = asb("S", "e_l", [128, 2], F32)
    e_sq = asb("S", "e_sq", [128, 2], F32)
    e_mx = asb("S", "e_mx", [128, 2], F32)
    e_mn = asb("S", "e_mn", [128, 2], F32)
    e_bC = asb("S", "e_bC", [128, 2], F32)
    e_bD = asb("S", "e_bD", [128, 2], F32)
    er_m = asb("S", "er_m", [1, 2], F32)
    er_i = asb("S", "er_i", [1, 2], F32)
    er_x = asb("S", "er_x", [1, 2], F32)
    er_n = asb("S", "er_n", [1, 2], F32)
    er_C = asb("S", "er_C", [1, 2], F32)
    er_D = asb("S", "er_D", [1, 2], F32)
    er_r1 = asb("S", "er_r1", [1, 2], F32)
    er_t = asb("S", "er_t", [1, 2], F32)

    outs_sb = [asb("F", f"outs{k}", [128, SC], BF16) for k in range(8)]
    xf2_p = [[asb("F", f"xf2_{p}_{k}", [128, SC], F32) for k in range(8)] for p in range(2)]
    x1_p = [[asb("F", f"x1_{p}_{k}", [128, SC], F32) for k in range(8)] for p in range(2)]
    xq_p = [[asb("F", f"xq_{p}_{k}", [128, SC + 2], BF16) for k in range(8)] for p in range(2)]
    g_sb = [asb("F", f"g_{i}", [128, SC], F32) for i in range(32)]
    h_pp = [asb("F", f"h_{p}", [128, SC + 2], F32) for p in range(4)]
    xq2_sb = [asb("F", f"xq2_{i}", [128, SC], BF16) for i in range(32)]
    y_sb = [asb("F", f"ysb{k}", [128, SC], F32) for k in range(8)]
    w2s_p = [asb("F", f"w2s{p}", [128, HID], BF16) for p in range(2)]
    w1s_p = [asb("F", f"w1s{p}", [128, D], BF16) for p in range(3)]
    bc_m = asb("F", "bc_m", [128, SC], F32)
    bc_l = asb("F", "bc_l", [128, SC], F32)
    bc_C = asb("F", "bc_C", [128, SC], F32)
    bc_D = asb("F", "bc_D", [128, SC], F32)
    bc_r1 = asb("F", "bc_r1", [128, SC + 2], F32)
    bc_s3 = asb("F", "bc_s3", [128, SC], F32)
    bc_r3 = asb("F", "bc_r3", [128, SC], F32)
    sq2 = asb("F", "sq2", [128, SC], F32)
    amx = [asb("F", f"amx{i}", [128, SC], F32) for i in range(8)]
    t_mx = asb("F", "t_mx", [128, SC], F32)
    t_mn = asb("F", "t_mn", [128, SC], F32)
    colB = asb("F", "colB", [128, 4], F32)
    r_m2 = asb("F", "r_m2", [1, SC], F32)
    r_i2 = asb("F", "r_i2", [1, SC], F32)
    r_mx = asb("F", "r_mx", [1, SC], F32)
    r_mn = asb("F", "r_mn", [1, SC], F32)
    r_C = asb("F", "r_C", [1, SC], F32)
    r_D = asb("F", "r_D", [1, SC], F32)
    r_r1 = [asb("F", f"r_r1{p}", [1, SC + 2], F32) for p in range(2)]
    r_s3 = asb("F", "r_s3", [1, SC], F32)
    r_r3 = asb("F", "r_r3", [1, SC], F32)
    r_t = asb("F", "r_t", [1, SC], F32)

    dma("sync", sSET, ident_s[:, :], ident_in[:, :])
    dma("sync", sSET, onesc_s[:, :], onesc_in[:, :])
    dma("sync", sSET, hmask_s[:, :], hmask_in[:, :])
    dma("sync", sSET, mgc_s[:, :], mgc_in[:, :])
    dma("sync", sSET, skip_s[:, :], skip2[:, :])
    dma("sync", sSET, nw_s[:, :], nw_sl[:, :])
    dma("sync", sSET, stage[:, 0:128], a_blk[:, :])
    dma("sync", sSET, stage[:, 128:256], b_blk[:, :])
    dma("sync", sSET, stage[:, 256:384], c_blk[:, :])
    for i in range(8):
        dma("sync", sSET, bout_s[:, i:i + 1], b_out_c[128 * i:128 * (i + 1), :])
    for i in range(32):
        dma("sync", sSET, cw_s[:, 3 * i:3 * i + 3], cw_in[128 * i:128 * (i + 1), :])
        dma("sync", sSET, cb_s[:, i:i + 1], cb_in[128 * i:128 * (i + 1), :])
    vABC = sSET.v

    emit("scalar", lambda a: a.activation(wA[:, :], stage[:, 0:128], Act.Tanh),
         wait=(sSET, vABC), inc=sA)
    emit("vector", lambda v: v.tensor_copy(wB[:, :], stage[:, 128:256]), wait=(sSET, vABC))
    emit("vector", lambda v: v.tensor_copy(wC[:, :], stage[:, 256:384]))
    emit("vector", lambda v: v.memset(z_p[0][:, :], 0.0))
    emit("vector", lambda v: v.memset(z_p[1][:, :], 0.0))

    wout_cast = []
    for k in range(8):
        w_ = [(sV, wout_cast[k - 2])] if k >= 2 else None
        vW = dma("sync", sWO[k % 2], xf_p[1][:, (k % 2) * D:(k % 2) * D + D],
                 w_outT[128 * k:128 * (k + 1), :], wait=w_)
        wout_cast.append(emit("vector", lambda v, k=k: v.tensor_copy(
            woutb[k][:, :], xf_p[1][:, (k % 2) * D:(k % 2) * D + D]),
            wait=(sWO[k % 2], vW), inc=sV))
    WOUT_DONE = wout_cast[-1]

    NT0 = RT // 128
    tdone = []
    for it in range(NT0):
        w_ = [(sV, tdone[it - 2])] if it >= 2 else None
        vX = dma("sync", sXT[it % 2], xt_p[it % 2][:, :],
                 x_tok[128 * it:128 * (it + 1), :], wait=w_)

        def fB(v, it=it, vX=vX):
            xt = xt_p[it % 2]
            v.reduce_sum(c_t1[:, :], xt[:, :], axis=AXX)._wait_ge(sXT[it % 2].h, vX)
            v.tensor_tensor(sq_s[:, :], xt[:, :], xt[:, :], Alu.mult)
            v.reduce_sum(c_t2[:, :], sq_s[:, :], axis=AXX)
            v.tensor_scalar(st_m[:, it:it + 1], c_t1[:, :], 1.0 / D, None, Alu.mult)
            v.tensor_scalar(c_t2[:, :], c_t2[:, :], 1.0 / D, None, Alu.mult)
            v.tensor_tensor(c_t3[:, :], st_m[:, it:it + 1], st_m[:, it:it + 1], Alu.mult)
            v.tensor_tensor(c_t2[:, :], c_t2[:, :], c_t3[:, :], Alu.subtract)
            v.tensor_scalar(c_t3[:, :], c_t2[:, :], 1e-5, None, Alu.add)
            v.reciprocal(st_r[:, it:it + 1], c_t3[:, :])
            v.tensor_tensor(c_t2[:, :], c_t2[:, :], st_r[:, it:it + 1], Alu.mult)
            v.tensor_scalar(c_t2[:, :], c_t2[:, :], 1e-4, None, Alu.add)
            return v.reciprocal(st_q[:, it:it + 1], c_t2[:, :])
        tdone.append(emit("vector", fB, inc=sV))
    aLN = emit("scalar", lambda a: a.activation(st_l[:, :], st_r[:, :], Act.Sqrt),
               wait=(sV, tdone[-1]), inc=sA)
    aRM = emit("scalar", lambda a: a.activation(st_q[:, :], st_q[:, :], Act.Sqrt), inc=sA)
    vSK = emit("vector", lambda v: v.tensor_tensor(st_s[:, :], st_l[:, :], st_q[:, :],
                                                   Alu.mult), wait=(sA, aRM), inc=sV)
    tap("str", [(st_r[:, :], F32, (128, 32), (sV, vSK))])
    tap("st", [(st_m[:, :], F32, (128, 32), (sV, vSK)),
               (st_l[:, :], F32, (128, 32), (sV, vSK)),
               (st_s[:, :], F32, (128, 32), (sV, vSK))])
    gp_st = []
    for j, st in enumerate((st_m, st_l, st_s)):
        w_ = [(sV, vSK)] if j == 0 else [(sST, gp_st[-1])]
        pT = pe_pb(0, lambda t, st=st: t.transpose(pb[0][0:32, 0:128], st[:, :],
                                                   ident_s[:, :]), wait=w_)
        vT = dv_pb(0, lambda v: v.tensor_copy(st_row[:, :], pb[0][0:32, 0:128]),
                   wait=(sP, pT))
        gp_st.append(dma("gpsimd", sST, ag_st_in[j], st_row[:, :], wait=[(sV, vT)]))
    ccST = emit("gpsimd", lambda g: g.collective_compute(
        "AllGather", Alu.bypass, replica_groups=[list(range(NCORE))],
        ins=[ag_st_in.opt()], outs=[ag_st_out.opt()]),
        wait=(sST, gp_st[-1]), inc=sC)

    xr_done = [None] * (NCH + 2)
    xf_done = [None] * (NCH + 2)
    u_done = [None] * (NCH + 1)
    mxe_done = [None] * NCH
    a2a_dmas = [None] * NCH
    wq = {"lastv": 0, "sw_g": 0, "w2_last": 0, "w1_last": 0}
    N_DMA_SYNC = [0]

    def emit_xr(k):
        w_ = []
        if k >= 3:
            w_.append((sV, xr_done[k - 3]))
        if k == 1:
            w_.append((sV, WOUT_DONE))
        dma("sync", sXF[k % 2], xf_p[k % 3][:, 0:CK], x_feat[:, CK * k:CK * (k + 1)],
            wait=w_ or None)
        d2 = dma("sync", sXF[k % 2], xf_p[k % 3][:, CK:2 * CK],
                 x_feat[:, T + CK * k:T + CK * (k + 1)])
        xf_done[k] = d2
        last_xr = None
        for b in range(2):
            R0 = b * T + CK * k
            blk, off = R0 // RT, R0 % RT
            w2_ = []
            if k == 0 and b == 0:
                w2_.append((sC, ccST))
            if last_xr is not None:
                w2_.append((sV, last_xr))
            elif k >= 1:
                w2_.append((sV, xr_done[k - 1]))
            repl(sRP, mbc[:, :], ag_st_out.tensor, ((blk * 3 + 0) * 32 * 128) + off, CK,
                 wait=w2_ or None)
            gv = repl(sRP, sbc[:, :], ag_st_out.tensor, ((blk * 3 + 2) * 32 * 128) + off, CK)
            cols = slice(b * CK, b * CK + CK)

            def fxr(v, k=k, cols=cols):
                xf = xf_p[k % 3]
                v.tensor_tensor(xf[:, cols], xf[:, cols], mbc[:, :], Alu.subtract)
                v.tensor_tensor(xf[:, cols], xf[:, cols], sbc[:, :], Alu.mult)
                return v.tensor_scalar(xr_p[k % 3][:, cols], xf[:, cols], nw_s[:, 0:1],
                                       None, Alu.mult)
            last_xr = emit("vector", fxr, wait=[(sXF[k % 2], d2), (sRP, gv)], inc=sV)
        xr_done[k] = last_xr

    def emit_u(k):
        last = None
        for jb in range(4):
            w_ = []
            if jb == 0:
                w_.append((sV, xr_done[k]))
            pv = pe_pb(jb % 2, lambda t, jb=jb, k=k: t.matmul(
                pb[jb % 2][:, 0:512], wB[:, :],
                xr_p[k % 3][:, 512 * jb:512 * (jb + 1)], start=True, stop=True),
                wait=w_ or None)
            w2_ = [(sP, pv)]
            if jb == 0 and k >= 2:
                w2_ += [(sTb[0], CK * (k - 1)), (sTb[1], CK * (k - 1))]
            last = dv_pb(jb % 2, lambda v, jb=jb, k=k: v.tensor_copy(
                u_p[k % 2][:, 512 * jb:512 * (jb + 1)], pb[jb % 2][:, 0:512]),
                wait=w2_)
        u_done[k] = last

    def emit_scan(k):
        wait("tensor", sV, u_done[k])
        wait("scalar", sV, u_done[k])
        for j in range(1, CK + 1):
            tp = CK * k + j
            for b in range(2):
                if j == 1:
                    src = (z_p[(k - 1) % 2][:, b * (CK + 1) + CK:b * (CK + 1) + CK + 1]
                           if k > 0 else z_p[0][:, b * (CK + 1):b * (CK + 1) + 1])
                else:
                    src = z_p[k % 2][:, b * (CK + 1) + j - 1:b * (CK + 1) + j]
                dst = z_p[k % 2][:, b * (CK + 1) + j:b * (CK + 1) + j + 1]
                bias = u_p[k % 2][:, b * CK + j - 1:b * CK + j]
                bank = pb_sc[2 * b + (tp % 2)]
                emit("tensor", lambda t, bank=bank, src=src: t.matmul(
                    bank[:, 0:1], wA[:, :], src, start=True, stop=True),
                    wait=(sTb[b], tp - 1), inc=sMb[b], fuse=True)
                emit("scalar", lambda a, bank=bank, dst=dst, bias=bias: a.activation(
                    dst, bank[:, 0:1], Act.Tanh, bias=bias),
                    wait=(sMb[b], tp), inc=sTb[b], fuse=True)

    def emit_mixed(k):
        last = None
        for b in range(2):
            for jb in range(2):
                x = 2 + ((2 * b + jb) % 2)
                w_ = []
                if b == 0 and jb == 0:
                    w_ += [(sTb[0], CK * (k + 1)), (sTb[1], CK * (k + 1))]
                pv = pe_pb(x, lambda t, k=k, b=b, jb=jb: t.matmul(
                    pb[2 + ((2 * b + jb) % 2)][:, 0:512], wC[:, :],
                    z_p[k % 2][:, b * (CK + 1) + 1 + 512 * jb:
                               b * (CK + 1) + 1 + 512 * (jb + 1)],
                    start=True, stop=True), wait=w_ or None)
                w2_ = [(sP, pv)]
                if b == 0 and jb == 0 and k >= 2:
                    w2_.append((sA2A[k % 2], a2a_dmas[k - 2]))

                def fmx(v, k=k, b=b, jb=jb, x=x):
                    c0 = b * CK + 512 * jb
                    return v.scalar_tensor_tensor(
                        mx_p[k % 2][:, c0:c0 + 512], xr_p[k % 3][:, c0:c0 + 512],
                        skip_s[:, 0:1], pb[x][:, 0:512], Alu.mult, Alu.add)
                last = dv_pb(x, fmx, wait=w2_)
        mxe_done[k] = last
        for b in range(2):
            blk = (0 if b == 0 else 4) + k // 4
            off = CK * (k % 4)
            a2a_dmas[k] = dma("gpsimd", sA2A[k % 2], a2a_in[blk, :, off:off + CK],
                              mx_p[k % 2][:, b * CK:b * CK + CK],
                              wait=[(sV, mxe_done[k])] if b == 0 else None)

    def _rsabs(v, out, in_, semh, val):
        i = v.reduce_sum(out, in_, axis=AXX, apply_absolute_value=True)
        i._wait_ge(semh, val)
        return i

    def _wq_w1abs(kk):
        w_ = [(sV, wq["lastv"])] if wq["lastv"] else None
        vL = dma("sync", sWQ, wq_ld[kk % 2][:, :],
                 w1T_in[128 * kk:128 * (kk + 1), :], wait=w_)
        wq["lastv"] = emit("vector", lambda v, kk=kk, vL=vL: _rsabs(
            v, wq_c1[:, kk:kk + 1], wq_ld[kk % 2][:, :], sWQ.h, vL), inc=sV)

    def _wq_w2abs(kk):
        vL = dma("sync", sWQ, wq_ld[kk % 2][:, 0:D],
                 w2T_in[128 * kk:128 * (kk + 1), :], wait=[(sV, wq["lastv"])])
        wq["lastv"] = emit("vector", lambda v, kk=kk, vL=vL: _rsabs(
            v, wq_c2[:, kk:kk + 1], wq_ld[kk % 2][:, 0:D], sWQ.h, vL), inc=sV)

    def _wq_scales(_):
        p1 = pe_pb(1, lambda t: t.matmul(pb[1][0:1, 0:8], onesc_s[:, :],
                                         wq_c1[:, :], start=True, stop=True),
                   wait=(sV, wq["lastv"]))
        p2 = pe_pb(1, lambda t: t.matmul(pb[1][0:1, 8:40], onesc_s[:, :],
                                         wq_c2[:, :], start=True, stop=True))

        def fsw(v):
            v.reduce_sum(sw1_row[:, :], pb[1][0:1, 0:8], axis=AXX)
            v.tensor_scalar(sw1_row[:, :], sw1_row[:, :], 1.0 / (D * HID), None, Alu.mult)
            v.tensor_scalar(sw1_row[:, :], sw1_row[:, :], 1e-5, None, Alu.max)
            v.reciprocal(sw1_row[:, :], sw1_row[:, :])
            v.reduce_sum(sw2_row[:, :], pb[1][0:1, 8:40], axis=AXX)
            v.tensor_scalar(sw2_row[:, :], sw2_row[:, :], 1.0 / (D * HID), None, Alu.mult)
            v.tensor_scalar(sw2_row[:, :], sw2_row[:, :], 1e-5, None, Alu.max)
            return v.reciprocal(sw2_row[:, :], sw2_row[:, :])
        wq["lastv"] = dv_pb(1, fsw, wait=[(sP, p2)])
        repl(sBC, sw1_c[:, :], sw1_row, 0, 1, wait=[(sV, wq["lastv"])], slot=0)
        wq["sw_g"] = repl(sBC, sw2_c[:, :], sw2_row, 0, 1, slot=1)
        wait("vector", sBC, wq["sw_g"])

    def _wq_w1q(kk):
        vL = dma("sync", sWQ, wq_ld[kk % 2][:, :],
                 w1T_in[128 * kk:128 * (kk + 1), :], wait=[(sV, wq["lastv"])])
        w_ = [(sWQG, wq["w1_last"])] if wq["w1_last"] else None

        def fq1(v, kk=kk, vL=vL):
            w = wq_ld[kk % 2]
            v.tensor_scalar(w[:, :], w[:, :], sw1_c[:, 0:1], None,
                            Alu.mult)._wait_ge(sWQ.h, vL)
            v.tensor_scalar(w[:, :], w[:, :], MAGIC, -MAGIC, Alu.add, Alu.add)
            return v.tensor_scalar(w1tile[:, :], w[:, :], 1.0, -1.0, Alu.min, Alu.max)
        wq["lastv"] = emit("vector", fq1, wait=w_, inc=sV)
        dst = bass.AP(w1q_dram.tensor, 128 * kk,
                      [[D, 128], [128 * D, 32], [1, 128]])
        src_ = w1tile[:, :].rearrange("p (i q) -> p i q", i=32, q=128)
        wq["w1_last"] = dma("gpsimd", sWQG, dst, src_, wait=[(sV, wq["lastv"])])

    def _wq_w2q(kk):
        vL = dma("sync", sWQ, wq_ld[kk % 2][:, 0:D],
                 w2T_in[128 * kk:128 * (kk + 1), :], wait=[(sV, wq["lastv"])])
        w_ = [(sWQG, wq["w2_last"])] if wq["w2_last"] else None

        def fq2(v, kk=kk, vL=vL):
            w = wq_ld[kk % 2]
            v.tensor_scalar(w[:, 0:D], w[:, 0:D], sw2_c[:, 0:1], None,
                            Alu.mult)._wait_ge(sWQ.h, vL)
            v.tensor_scalar(w[:, 0:D], w[:, 0:D], MAGIC, -MAGIC, Alu.add, Alu.add)
            return v.tensor_scalar(w2tile[:, :], w[:, 0:D], 1.0, -1.0,
                                   Alu.min, Alu.max)
        wq["lastv"] = emit("vector", fq2, wait=w_, inc=sV)
        dst = bass.AP(w2q_dram.tensor, 128 * kk,
                      [[HID, 128], [128 * HID, 8], [1, 128]])
        src_ = w2tile[:, :].rearrange("p (i q) -> p i q", i=8, q=128)
        wq["w2_last"] = dma("gpsimd", sWQG, dst, src_, wait=[(sV, wq["lastv"])])

    wq_tasks = ([("w1abs", kk) for kk in range(8)]
                + [("w2abs", kk) for kk in range(32)]
                + [("scales", 0)]
                + [("w1q", kk) for kk in range(8)]
                + [("w2q", kk) for kk in range(32)])
    _WQ_FN = {"w1abs": _wq_w1abs, "w2abs": _wq_w2abs, "scales": _wq_scales,
              "w1q": _wq_w1q, "w2q": _wq_w2q}

    def emit_wq_slice(k):
        n = 6 if k < NCH - 1 else len(wq_tasks)
        for _ in range(n):
            if not wq_tasks:
                return
            kind, kk = wq_tasks.pop(0)
            _WQ_FN[kind](kk)

    emit_xr(0)
    emit_xr(1)
    emit_u(0)
    tap("wA", [(wA[:, :], BF16, (128, 128), (sA, 1))])
    tap("xr0", [(xr_p[0][:, :], BF16, (128, 2 * CK), (sV, xr_done[0]))])
    tap("u0", [(u_p[0][:, :], F32, (128, 2 * CK), (sV, u_done[0]))])
    for k in range(NCH):
        if k + 2 < NCH:
            emit_xr(k + 2)
        emit_scan(k)
        if k == 0:
            tap("z0", [(z_p[0][:, :], BF16, (128, 2 * (CK + 1)),
                        [(sT0, CK), (sT1, CK)])])
        if k + 1 < NCH:
            emit_u(k + 1)
        emit_mixed(k)
        if k == 0:
            tap("mx0", [(mx_p[0][:, :], BF16, (128, 2 * CK), (sV, mxe_done[0]))])
        emit_wq_slice(k)
    tap("sw", [(sw1_c[:, :], F32, (128, 1), (sBC, wq["sw_g"])),
               (sw2_c[:, :], F32, (128, 1), (sBC, wq["sw_g"]))])

    def _a2a(g):
        with allow_ici_all_to_all():
            return g.collective_compute("AllToAll", Alu.bypass,
                                        replica_groups=[list(range(NCORE))],
                                        ins=[a2a_in.opt()], outs=[a2a_out.opt()])
    ccA2A = emit("gpsimd", _a2a, wait=(sA2A[(NCH - 1) % 2], a2a_dmas[-1]), inc=sC)

    eo_last = None
    for kk in range(8):
        eo_last = dma("sync", sEO, e_outs[kk][:, 0:1], a2a_out[kk, :, 0:1],
                      wait=[(sC, ccA2A)] if kk == 0 else None, slow=True)
        eo_last = dma("sync", sEO, e_outs[kk][:, 1:2], a2a_out[kk, :, RT - 1:RT],
                      slow=True)
        eo_last = dma("sync", sEO, e_x[kk][:, :], x_edge[128 * kk:128 * (kk + 1), :])
    repl(sEM, e_m[:, 0:1], ag_st_in.tensor, 0, 1,
         wait=[(sA2A[(NCH - 1) % 2], a2a_dmas[-1])])
    repl(sEM, e_m[:, 1:2], ag_st_in.tensor, RT - 1, 1)
    repl(sEM, e_l[:, 0:1], ag_st_in.tensor, RT, 1)
    gml = repl(sEM, e_l[:, 1:2], ag_st_in.tensor, 2 * RT - 1, 1)

    ep = []
    for i in range(8):
        pv = None
        for kk in range(8):
            w_ = []
            if i == 0 and kk == 0:
                w_ = [(sEO, eo_last)]
            pv = pe_pb(i % 2, lambda t, i=i, kk=kk: t.matmul(
                pb[i % 2][:, 0:2], woutb[kk][:, 128 * i:128 * (i + 1)],
                e_outs[kk][:, :], start=(kk == 0), stop=(kk == 7)), wait=w_ or None)

        def fex(v, i=i):
            v.tensor_tensor(e_sq[:, :], e_x[i][:, :], e_m[:, :], Alu.subtract)
            v.tensor_tensor(e_sq[:, :], e_sq[:, :], e_l[:, :], Alu.mult)
            v.tensor_tensor(e_sq[:, :], e_sq[:, :], e_x[i][:, :], Alu.add)
            return v.scalar_tensor_tensor(e_x1[i][:, :], pb[i % 2][:, 0:2],
                                          bout_s[:, i:i + 1], e_sq[:, :], Alu.add, Alu.add)
        ep.append(dv_pb(i % 2, fex, wait=[(sEM, gml), (sP, pv)]))
    pst = None
    for i in range(8):
        pst = pe_pb(2, lambda t, i=i: t.matmul(
            pb[2][0:1, 0:2], onesc_s[:, :], e_x1[i][:, :], start=(i == 0), stop=(i == 7)),
            wait=[(sV, ep[-1])] if i == 0 else None)
    pst2 = None
    for i in range(8):
        vq = emit("vector", lambda v, i=i: v.tensor_tensor(
            e_sq[:, :], e_x1[i][:, :], e_x1[i][:, :], Alu.mult),
            wait=[(sP, pst2)] if pst2 else None, inc=sV)
        pst2 = pe_pb(3, lambda t, i=i: t.matmul(
            pb[3][0:1, 0:2], onesc_s[:, :], e_sq[:, :], start=(i == 0), stop=(i == 7)),
            wait=(sV, vq))

    def fmm(v):
        v.tensor_copy(e_mx[:, :], e_x1[0][:, :])
        v.tensor_copy(e_mn[:, :], e_x1[0][:, :])
        for i in range(1, 8):
            v.tensor_tensor(e_mx[:, :], e_mx[:, :], e_x1[i][:, :], Alu.max)
            v.tensor_tensor(e_mn[:, :], e_mn[:, :], e_x1[i][:, :], Alu.min)
        return v.engine_nop()
    vmm = emit("vector", fmm, inc=sV)
    last = vmm
    for (srcb, dstrow, op) in ((e_mx, er_x, Alu.max), (e_mn, er_n, Alu.min)):
        for e in range(2):
            pT = pe_pb(0, lambda t, srcb=srcb, e=e: t.transpose(
                pb[0][0:1, 0:128], srcb[:, e:e + 1], ident_s[:, :]), wait=[(sV, last)])
            last = dv_pb(0, lambda v, dstrow=dstrow, e=e, op=op: v.tensor_reduce(
                dstrow[:, e:e + 1], pb[0][0:1, 0:128], AXX, op), wait=(sP, pT))

    def fer(v):
        v.tensor_copy(er_m[:, :], pb[2][0:1, 0:2])
        v.tensor_scalar(er_m[:, :], er_m[:, :], 1.0 / D, None, Alu.mult)
        v.tensor_copy(er_i[:, :], pb[3][0:1, 0:2])
        v.tensor_scalar(er_i[:, :], er_i[:, :], 1.0 / D, None, Alu.mult)
        v.tensor_tensor(er_t[:, :], er_m[:, :], er_m[:, :], Alu.mult)
        v.tensor_tensor(er_i[:, :], er_i[:, :], er_t[:, :], Alu.subtract)
        v.tensor_scalar(er_i[:, :], er_i[:, :], 1e-5, None, Alu.add)
        return v.reciprocal(er_i[:, :], er_i[:, :])
    vr1 = dv_pb(2, fer, wait=[(sP, pst2), (sV, last)])
    pbr[3] = vr1
    ar1 = emit("scalar", lambda a: a.activation(er_i[:, :], er_i[:, :], Act.Sqrt),
               wait=(sV, vr1), inc=sA)

    def fer2(v):
        v.tensor_tensor(er_x[:, :], er_x[:, :], er_m[:, :], Alu.subtract)
        v.tensor_tensor(er_n[:, :], er_m[:, :], er_n[:, :], Alu.subtract)
        v.tensor_tensor(er_x[:, :], er_x[:, :], er_n[:, :], Alu.max)
        v.tensor_tensor(er_x[:, :], er_x[:, :], er_i[:, :], Alu.mult)
        v.tensor_scalar(er_x[:, :], er_x[:, :], 1e-5, None, Alu.max)
        v.reciprocal(er_x[:, :], er_x[:, :])
        v.tensor_scalar(er_x[:, :], er_x[:, :], 127.0, None, Alu.mult)
        v.tensor_tensor(er_C[:, :], er_i[:, :], er_x[:, :], Alu.mult)
        v.tensor_tensor(er_D[:, :], er_m[:, :], er_C[:, :], Alu.mult)
        v.tensor_scalar(er_D[:, :], er_D[:, :], -1.0, None, Alu.mult)
        v.tensor_scalar(er_r1[:, :], er_x[:, :], sw1_row[0:1, 0:1], None, Alu.mult)
        return v.reciprocal(er_r1[:, :], er_r1[:, :])
    vr2 = emit("vector", fer2, wait=(sA, ar1), inc=sV)
    repl(sBC, e_bC[:, :], er_C, 0, 2, wait=[(sV, vr2)], slot=2)
    gD = repl(sBC, e_bD[:, :], er_D, 0, 2, slot=3)
    exq_last = None
    for i in range(8):
        def fxq(v, i=i):
            i1 = v.tensor_tensor(e_sq[:, :], e_x1[i][:, :], e_bC[:, :], Alu.mult)
            i1._wait_ge(sBC.h, gD)
            v.tensor_tensor(e_sq[:, :], e_sq[:, :], e_bD[:, :], Alu.add)
            return v.tensor_scalar(e_xq[i][:, :], e_sq[:, :], MAGIC, -MAGIC,
                                   Alu.add, Alu.add)
        vq = emit("vector", fxq, inc=sV)
        exq_last = dma("gpsimd", sEGW, egA_in[128 * i:128 * (i + 1), :], e_xq[i][:, :],
                       wait=[(sV, vq)])
    gB = dma("gpsimd", sEGW, egB_in[:, :], er_r1[:, :], wait=[(sV, vr2)])
    ccEA = emit("gpsimd", lambda g: g.collective_compute(
        "AllGather", Alu.bypass, replica_groups=[list(range(NCORE))],
        ins=[egA_in.opt()], outs=[egA_out.opt()]), wait=(sEGW, gB), inc=sC)
    ccEB = emit("gpsimd", lambda g: g.collective_compute(
        "AllGather", Alu.bypass, replica_groups=[list(range(NCORE))],
        ins=[egB_in.opt()], outs=[egB_out.opt()]), wait=(sEGW, gB), inc=sC)

    def fhalo(g):
        g.wait_ge(sC.h, ccEB)
        with g.register("rl") as rl, g.register("rr") as rr:
            pid = g.partition_id()
            g.reg_mov(rl, pid)
            g.reg_add(rl, rl, 7)
            g.reg_mod(rl, rl, 8)
            g.reg_mul(rl, rl, D * 2)
            g.reg_add(rl, rl, 1)
            g.reg_mov(rr, pid)
            g.reg_add(rr, rr, 1)
            g.reg_mod(rr, rr, 8)
            g.reg_mul(rr, rr, D * 2)
            for kk in range(8):
                g.dma_start(out=xqh_l[kk][:, :],
                            in_=bass.AP(egA_out.tensor, rl, [[2, 128], [1, 1]]),
                            allow_slow_non_contiguous=True).then_inc(sHL.h, 16)
                g.reg_add(rl, rl, 256)
                g.dma_start(out=xqh_r[kk][:, :],
                            in_=bass.AP(egA_out.tensor, rr, [[2, 128], [1, 1]]),
                            allow_slow_non_contiguous=True).then_inc(sHL.h, 16)
                g.reg_add(rr, rr, 256)
            g.reg_mov(rl, pid)
            g.reg_add(rl, rl, 7)
            g.reg_mod(rl, rl, 8)
            g.reg_mul(rl, rl, 2)
            g.reg_add(rl, rl, 1)
            g.reg_mov(rr, pid)
            g.reg_add(rr, rr, 1)
            g.reg_mod(rr, rr, 8)
            g.reg_mul(rr, rr, 2)
            g.dma_start(out=r1h[:, 0:1],
                        in_=bass.AP(egB_out.tensor, rl, [[1, 1], [1, 1]])
                        ).then_inc(sHL.h, 16)
            return g.dma_start(out=r1h[:, 1:2],
                               in_=bass.AP(egB_out.tensor, rr, [[1, 1], [1, 1]]))
    emit("gpsimd", fhalo, inc=sHL, inc_by=16)
    sHL.bump(16 * 17)
    gH = sHL.v
    vHM = None
    for kk in range(8):
        def fm(v, kk=kk):
            i1 = v.tensor_tensor(xqh_l[kk][:, :], xqh_l[kk][:, :], hmask_s[:, 0:1],
                                 Alu.mult)
            if kk == 0:
                i1._wait_ge(sHL.h, gH)
            v.tensor_tensor(xqh_r[kk][:, :], xqh_r[kk][:, :], hmask_s[:, 1:2], Alu.mult)
            return v.engine_nop()
        vHM = emit("vector", fm, inc=sV)

    u2a = [dict() for _ in range(NSC)]
    u2b = [dict() for _ in range(NSC)]

    def tree_tr(bank, dst_max, dst_min, wait_first):
        srcs = [(t_mx, 0), (t_mx, 1), (t_mn, 0), (t_mn, 1)]
        pT = None
        for q, (srcb, half) in enumerate(srcs):
            pT = pe_pb(bank, lambda t, q=q, srcb=srcb, half=half: t.transpose(
                pb[bank][:, 128 * q:128 * (q + 1)],
                srcb[:, 128 * half:128 * (half + 1)], ident_s[:, :]),
                wait=wait_first if q == 0 else None)

        def fred(v, bank=bank):
            for q in range(4):
                v.tensor_reduce(colB[:, q:q + 1], pb[bank][:, 128 * q:128 * (q + 1)],
                                AXX, Alu.max if q < 2 else Alu.min)
            return v.engine_nop()
        rv = dv_pb(bank, fred, wait=(sP, pT))
        pT2 = None
        for q in range(4):
            pT2 = pe_pb(bank, lambda t, bank=bank, q=q: t.transpose(
                pb[bank][0:1, 128 * q:128 * (q + 1)], colB[:, q:q + 1],
                ident_s[:, :]), wait=[(sV, rv)] if q == 0 else None)

        def fcpy(v, bank=bank, dst_max=dst_max, dst_min=dst_min):
            v.tensor_copy(dst_max[:, 0:128], pb[bank][0:1, 0:128])
            v.tensor_copy(dst_max[:, 128:256], pb[bank][0:1, 128:256])
            v.tensor_copy(dst_min[:, 0:128], pb[bank][0:1, 256:384])
            v.tensor_copy(dst_min[:, 128:256], pb[bank][0:1, 384:512])
            return v.engine_nop()
        return dv_pb(bank, fcpy, wait=(sP, pT2))

    def emit_2a(j):
        p = j % 2
        st = u2a[j]
        od = None
        for kk in range(8):
            w_ = []
            if kk == 0:
                if j == 0:
                    w_ += [(sC, ccEB), (sV, vHM), (sV, wq["lastv"]),
                           (sWQG, wq["w2_last"]), (sWQG, wq["w1_last"])]
                else:
                    w_.append((sP, u2a[j - 1]["proj_done"]))
            od = dma("sync", sOUT, outs_sb[kk][:, :], a2a_out[kk, :, SC * j:SC * (j + 1)],
                     wait=w_ or None)
        st["od"] = od
        xd = None
        for kk in range(8):
            w_ = []
            if kk == 0 and j >= 2:
                w_.append((sV, u2b[j - 2]["fam"]))
            xd = dma("sync", sXF2[p], xf2_p[p][kk][:, :],
                     x_ftok[128 * kk:128 * (kk + 1), SC * j:SC * (j + 1)],
                     wait=w_ or None)
        w_ = [(sV, u2a[j - 1]["x1_done"])] if j >= 1 else None
        repl(sBC, bc_m[:, :], ag_st_in.tensor, SC * j, SC, wait=w_)
        gml2 = repl(sBC, bc_l[:, :], ag_st_in.tensor, RT + SC * j, SC)

        def fcmb(v):
            v.tensor_tensor(bc_m[:, :], bc_m[:, :], bc_l[:, :], Alu.mult)
            return v.tensor_scalar(bc_l[:, :], bc_l[:, :], 1.0, None, Alu.add)
        vcmb = emit("vector", fcmb, wait=[(sBC, gml2)], inc=sV)

        def fx1m(v, p=p):
            for i in range(8):
                v.tensor_tensor(xf2_p[p][i][:, :], xf2_p[p][i][:, :],
                                bc_l[:, :], Alu.mult)
            return v.engine_nop()
        emit("vector", fx1m, wait=[(sXF2[p], xd), (sV, vcmb)])

        def fx1s(v, p=p):
            for i in range(8):
                v.tensor_tensor(xf2_p[p][i][:, :], xf2_p[p][i][:, :],
                                bc_m[:, :], Alu.subtract)
            return v.engine_nop()
        emit("vector", fx1s)

        pvh = [None, None]
        x1d = None
        for half in range(2):
            for i in range(4 * half, 4 * half + 4):
                pv = None
                for kk in range(8):
                    w_ = []
                    if i == 0 and kk == 0:
                        w_.append((sOUT, od))
                    pv = pe_pb(i % 4, lambda t, i=i, kk=kk: t.matmul(
                        pb[i % 4][:, 0:SC], woutb[kk][:, 128 * i:128 * (i + 1)],
                        outs_sb[kk][:, :], start=(kk == 0), stop=(kk == 7)),
                        wait=w_ or None)
            pvh[half] = pv

            def fx1c(v, half=half, p=p):
                for i in range(4 * half, 4 * half + 4):
                    v.scalar_tensor_tensor(x1_p[p][i][:, :], pb[i % 4][:, 0:SC],
                                           bout_s[:, i:i + 1], xf2_p[p][i][:, :],
                                           Alu.add, Alu.add)
                return v.engine_nop()
            x1d = emit("vector", fx1c, wait=[(sP, pv)], inc=sV)
            for bk in range(4):
                pbr[bk] = x1d
        st["proj_done"] = pvh[1]
        st["x1_done"] = x1d

        for i in range(8):
            pe_pb(2, lambda t, i=i, p=p: t.matmul(
                pb[2][0:1, 0:SC], onesc_s[:, :], x1_p[p][i][:, :],
                start=(i == 0), stop=(i == 7)),
                wait=[(sV, x1d)] if i == 0 else None)
        pms = CNT["sP"]
        pmq = None
        for i in range(8):
            vq = emit("scalar", lambda a, i=i, p=p: a.activation(
                sq2[:, :], x1_p[p][i][:, :], Act.Square),
                wait=[(sP, pmq)] if pmq else [(sV, x1d)], inc=sA)
            pmq = pe_pb(3, lambda t, i=i: t.matmul(
                pb[3][0:1, 0:SC], onesc_s[:, :], sq2[:, :],
                start=(i == 0), stop=(i == 7)), wait=(sA, vq))

        def fmt(v, p=p):
            xs = x1_p[p]
            for q in range(4):
                v.tensor_tensor(amx[q][:, :], xs[2 * q][:, :], xs[2 * q + 1][:, :],
                                Alu.max)
                v.tensor_tensor(amx[4 + q][:, :], xs[2 * q][:, :],
                                xs[2 * q + 1][:, :], Alu.min)
            v.tensor_tensor(amx[0][:, :], amx[0][:, :], amx[1][:, :], Alu.max)
            v.tensor_tensor(amx[2][:, :], amx[2][:, :], amx[3][:, :], Alu.max)
            v.tensor_tensor(amx[4][:, :], amx[4][:, :], amx[5][:, :], Alu.min)
            v.tensor_tensor(amx[6][:, :], amx[6][:, :], amx[7][:, :], Alu.min)
            v.tensor_tensor(t_mx[:, :], amx[0][:, :], amx[2][:, :], Alu.max)
            v.tensor_tensor(t_mn[:, :], amx[4][:, :], amx[6][:, :], Alu.min)
            return v.engine_nop()
        last = emit("vector", fmt, inc=sV)
        last = tree_tr(0, r_mx, r_mn, [(sV, last)])

        def frow(v):
            v.tensor_copy(r_m2[:, :], pb[2][0:1, 0:SC])
            v.tensor_scalar(r_m2[:, :], r_m2[:, :], 1.0 / D, None, Alu.mult)
            v.tensor_copy(r_i2[:, :], pb[3][0:1, 0:SC])
            v.tensor_scalar(r_i2[:, :], r_i2[:, :], 1.0 / D, None, Alu.mult)
            v.tensor_tensor(r_t[:, :], r_m2[:, :], r_m2[:, :], Alu.mult)
            v.tensor_tensor(r_i2[:, :], r_i2[:, :], r_t[:, :], Alu.subtract)
            v.tensor_scalar(r_i2[:, :], r_i2[:, :], 1e-5, None, Alu.add)
            return v.reciprocal(r_i2[:, :], r_i2[:, :])
        vrw = dv_pb(2, frow, wait=[(sP, pmq), (sV, last)])
        pbr[3] = vrw
        arw = emit("scalar", lambda a: a.activation(r_i2[:, :], r_i2[:, :], Act.Sqrt),
                   wait=(sV, vrw), inc=sA)

        def frow2(v, p=p):
            v.tensor_tensor(r_mx[:, :], r_mx[:, :], r_m2[:, :], Alu.subtract)
            v.tensor_tensor(r_mn[:, :], r_m2[:, :], r_mn[:, :], Alu.subtract)
            v.tensor_tensor(r_mx[:, :], r_mx[:, :], r_mn[:, :], Alu.max)
            v.tensor_tensor(r_mx[:, :], r_mx[:, :], r_i2[:, :], Alu.mult)
            v.tensor_scalar(r_mx[:, :], r_mx[:, :], 1e-5, None, Alu.max)
            v.reciprocal(r_mx[:, :], r_mx[:, :])
            v.tensor_scalar(r_mx[:, :], r_mx[:, :], 127.0, None, Alu.mult)
            v.tensor_tensor(r_C[:, :], r_i2[:, :], r_mx[:, :], Alu.mult)
            v.tensor_tensor(r_D[:, :], r_m2[:, :], r_C[:, :], Alu.mult)
            v.tensor_scalar(r_D[:, :], r_D[:, :], -1.0, None, Alu.mult)
            v.tensor_scalar(r_t[:, :], r_mx[:, :], sw1_row[0:1, 0:1], None, Alu.mult)
            v.reciprocal(r_t[:, :], r_t[:, :])
            return v.tensor_copy(r_r1[p][:, 1:SC + 1], r_t[:, :])
        w_ = [(sA, arw)]
        if j >= 2:
            w_.append((sBC, u2b[j - 2]["gr1"]))
        vr2_ = emit("vector", frow2, wait=w_, inc=sV)
        w_ = [(sV, vr2_)]
        if j >= 1:
            w_.append((sV, u2a[j - 1]["xq_done"]))
        repl(sBC, bc_C[:, :], r_C, 0, SC, wait=w_, slot=2)
        gD_ = repl(sBC, bc_D[:, :], r_D, 0, SC, slot=3)

        w_ = [(sBC, gD_)]
        if j >= 2:
            w_.append((sP, u2b[j - 2]["mm1_done"]))

        def fqa(v, p=p):
            for i in range(8):
                v.tensor_tensor(xf2_p[p][i][:, :], x1_p[p][i][:, :],
                                bc_C[:, :], Alu.mult)
            return v.engine_nop()
        emit("vector", fqa, wait=w_)

        def fqb(v, p=p):
            for i in range(8):
                v.tensor_tensor(xf2_p[p][i][:, :], xf2_p[p][i][:, :],
                                bc_D[:, :], Alu.add)
            return v.engine_nop()
        emit("vector", fqb)

        def fqr(v, p=p):
            for i in range(8):
                v.tensor_scalar(xq_p[p][i][:, 1:SC + 1], xf2_p[p][i][:, :],
                                MAGIC, -MAGIC, Alu.add, Alu.add)
            return v.engine_nop()
        xql = emit("vector", fqr, inc=sV)
        st["xq_scr"] = xql

        def fhal(v, j=j, p=p):
            for i in range(8):
                if j == 0:
                    v.tensor_copy(xq_p[p][i][:, 0:1], xqh_l[i][:, :])
                else:
                    v.tensor_copy(xq_p[p][i][:, 0:1], xq_p[1 - p][i][:, SC:SC + 1])
            if j == 0:
                v.tensor_copy(r_r1[p][:, 0:1], r1h[:, 0:1])
            else:
                v.tensor_copy(r_r1[p][:, 0:1], r_r1[1 - p][:, SC:SC + 1])
            return v.engine_nop()
        w_ = [(sV, xql)]
        if j == 0:
            w_.append((sV, vHM))
        st["xq_done"] = emit("vector", fhal, wait=w_, inc=sV)

    def emit_2b(j):
        p = j % 2
        st = u2b[j]

        def frh(v, j=j, p=p):
            for i in range(8):
                if j == NSC - 1:
                    v.tensor_copy(xq_p[p][i][:, SC + 1:SC + 2], xqh_r[i][:, :])
                else:
                    v.tensor_copy(xq_p[p][i][:, SC + 1:SC + 2], xq_p[1 - p][i][:, 1:2])
            if j == NSC - 1:
                v.tensor_copy(r_r1[p][:, SC + 1:SC + 2], r1h[:, 1:2])
            else:
                v.tensor_copy(r_r1[p][:, SC + 1:SC + 2], r_r1[1 - p][:, 1:2])
            return v.engine_nop()
        jj = j if j == NSC - 1 else j + 1
        w_ = [(sV, u2a[jj]["xq_done"])]
        vrh = emit("vector", frh, wait=w_, inc=sV)
        w_ = [(sV, vrh)]
        if j >= 1:
            w_.append((sV, u2b[j - 1]["hev_last"]))
        gr1 = repl(sBC, bc_r1[:, :], r_r1[p], 0, SC + 2, wait=w_, slot=4)
        st["gr1"] = gr1

        hev = None
        sil = None
        sil_t = [None] * 32
        for i in range(32):
            bk = 4 + (i % 4)
            if i < 3:
                if j == 0:
                    wsl = [(sWQG, wq["w1_last"])] if i == 0 else None
                else:
                    wsl = [(sP, u2b[j - 1][f"mm1_{[30, 31, 29][i]}"])]
            else:
                wsl = [(sP, st[f"mm1_{i - 3}"])]
            sd1 = dma("gpsimd" if i % 2 == 0 else "sync", sW1S[i % 2],
                      w1s_p[i % 3][:, :], w1q_dram[i], wait=wsl)
            pv = None
            for kk in range(8):
                w_ = []
                if kk == 0:
                    w_.append((sW1S[i % 2], sd1))
                    if i == 0:
                        w_.append((sV, vrh))
                def fmm1(t, i=i, kk=kk, p=p, bk=bk):
                    return t.matmul(
                        pb_sc[bk - 4][:, 0:SC + 2],
                        w1s_p[i % 3][:, 128 * kk:128 * (kk + 1)],
                        xq_p[p][kk][:, :], start=(kk == 0), stop=(kk == 7))
                wt = list(w_)
                if kk == 0 and pbr[bk]:
                    wt.append((sV, pbr[bk]))
                pv = emit("tensor", fmm1, wait=wt or None, inc=sP)
            st[f"mm1_{i}"] = pv
            if i % 4 == 3:
                i0 = i - 3

                def fgrp(v, i0=i0, pv=pv, gr1=gr1):
                    for g in range(4):
                        i1 = v.tensor_tensor(h_pp[g][:, :], pb_sc[g][:, 0:SC + 2],
                                             bc_r1[:, :], Alu.mult)
                        if g == 0:
                            i1._wait_ge(sBC.h, gr1)
                    for g in range(4):
                        v.tensor_scalar(g_sb[i0 + g][:, :], h_pp[g][:, 1:SC + 1],
                                        cw_s[:, 3 * (i0 + g) + 1:3 * (i0 + g) + 2],
                                        None, Alu.mult)
                    for g in range(4):
                        v.scalar_tensor_tensor(g_sb[i0 + g][:, :], h_pp[g][:, 0:SC],
                                               cw_s[:, 3 * (i0 + g):3 * (i0 + g) + 1],
                                               g_sb[i0 + g][:, :], Alu.mult, Alu.add)
                    for g in range(4):
                        v.scalar_tensor_tensor(g_sb[i0 + g][:, :], h_pp[g][:, 2:SC + 2],
                                               cw_s[:, 3 * (i0 + g) + 2:3 * (i0 + g) + 3],
                                               g_sb[i0 + g][:, :], Alu.mult, Alu.add)
                    return v.engine_nop()
                wg = [(sP, pv)]
                if i == 3 and j >= 1:
                    wg.append((sA, u2b[j - 1]["q2_act"]))
                hev = emit("vector", fgrp, wait=wg, inc=sV)
                for g in range(4):
                    pbr[4 + g] = hev
                if j == 0 and i0 == 0:
                    snapcpy("w1s", w1s_p[0][:, :])
                    snapcpy("xqp", xq_p[p][0][:, :])
                    snapcpy("h0", h_pp[0][:, :])
                for g in range(4):
                    sil = emit("scalar", lambda a, i2_=i0 + g: a.activation(
                        g_sb[i2_][:, :], g_sb[i2_][:, :], Act.Silu,
                        bias=cb_s[:, i2_:i2_ + 1]), wait=(sV, hev) if g == 0 else None,
                        inc=sA)
                    sil_t[i0 + g] = sil
                if j == 0 and i0 == 0:
                    snapcpy("g0", g_sb[0][:, :], wait=(sA, sil))
        st["mm1_done"] = CNT["sP"]
        st["hev_last"] = hev

        xmn = xf2_p[p]

        def famA(v, p=p):
            for c in range(8):
                v.tensor_tensor(amx[c][:, :], g_sb[c][:, :], g_sb[8 + c][:, :],
                                Alu.max)
                v.tensor_tensor(xmn[c][:, :], g_sb[c][:, :], g_sb[8 + c][:, :],
                                Alu.min)
            return v.engine_nop()
        emit("vector", famA, wait=(sA, sil_t[15]))

        def famB(v, p=p):
            for c in range(8):
                v.tensor_tensor(amx[c][:, :], amx[c][:, :], g_sb[16 + c][:, :],
                                Alu.max)
                v.tensor_tensor(xmn[c][:, :], xmn[c][:, :], g_sb[16 + c][:, :],
                                Alu.min)
            for c in range(8):
                v.tensor_tensor(amx[c][:, :], amx[c][:, :], g_sb[24 + c][:, :],
                                Alu.max)
                v.tensor_tensor(xmn[c][:, :], xmn[c][:, :], g_sb[24 + c][:, :],
                                Alu.min)
            for c in range(4):
                v.tensor_tensor(amx[2 * c][:, :], amx[2 * c][:, :],
                                amx[2 * c + 1][:, :], Alu.max)
                v.tensor_tensor(xmn[2 * c][:, :], xmn[2 * c][:, :],
                                xmn[2 * c + 1][:, :], Alu.min)
            for c in range(2):
                v.tensor_tensor(amx[4 * c][:, :], amx[4 * c][:, :],
                                amx[4 * c + 2][:, :], Alu.max)
                v.tensor_tensor(xmn[4 * c][:, :], xmn[4 * c][:, :],
                                xmn[4 * c + 2][:, :], Alu.min)
            v.tensor_tensor(t_mx[:, :], amx[0][:, :], amx[4][:, :], Alu.max)
            v.tensor_tensor(t_mn[:, :], xmn[0][:, :], xmn[4][:, :], Alu.min)
            return v.engine_nop()
        last = emit("vector", famB, wait=(sA, sil_t[31]), inc=sV)
        st["fam"] = last
        last = tree_tr(2, r_s3, r_t, [(sV, last)])

        def fs3(v):
            v.tensor_scalar(r_t[:, :], r_t[:, :], -1.0, None, Alu.mult)
            v.tensor_tensor(r_s3[:, :], r_s3[:, :], r_t[:, :], Alu.max)
            v.tensor_scalar(r_s3[:, :], r_s3[:, :], 1e-5, None, Alu.max)
            v.reciprocal(r_s3[:, :], r_s3[:, :])
            v.tensor_scalar(r_s3[:, :], r_s3[:, :], 127.0, None, Alu.mult)
            v.tensor_scalar(r_r3[:, :], r_s3[:, :], sw2_row[0:1, 0:1], None, Alu.mult)
            return v.reciprocal(r_r3[:, :], r_r3[:, :])
        vs3 = emit("vector", fs3, wait=(sV, last), inc=sV)
        w_ = [(sV, vs3)]
        if j >= 1:
            w_.append((sV, u2b[j - 1]["q2_last"]))
        gs3 = repl(sBC, bc_s3[:, :], r_s3, 0, SC, wait=w_, slot=5)
        w2_ = [(sV, u2b[j - 1]["yev_last"])] if j >= 1 else None
        gr3 = repl(sBC, bc_r3[:, :], r_r3, 0, SC, wait=w2_, slot=6)

        q2l = [None] * 4
        for bq in range(4):
            def fq2m(v, bq=bq):
                for i in range(8 * bq, 8 * bq + 8):
                    v.tensor_tensor(g_sb[i][:, :], g_sb[i][:, :], bc_s3[:, :],
                                    Alu.mult)
                return v.engine_nop()
            q2l[bq] = emit("vector", fq2m,
                           wait=[(sBC, gs3)] if bq == 0 else None, inc=sV)
        st["q2_last"] = q2l[3]

        q2a = [None] * 4
        for bq in range(4):
            w_ = [(sV, q2l[bq])]
            if bq == 0 and j >= 1:
                w_.append((sP, u2b[j - 1]["mm2_done"]))

            def fr2(a, bq=bq):
                for g in range(8 * bq, 8 * bq + 8):
                    a.activation(g_sb[g][:, :], g_sb[g][:, :],
                                 Act.Identity, bias=mgc_s[:, 0:1])
                for g in range(8 * bq, 8 * bq + 8):
                    i_ = a.activation(xq2_sb[g][:, :], g_sb[g][:, :],
                                      Act.Identity, bias=mgc_s[:, 1:2])
                return i_
            q2a[bq] = emit("scalar", fr2, wait=w_, inc=sA)
        st["q2_act"] = q2a[3]
        if j == 0:
            snapcpy("xq20", xq2_sb[0][:, :], wait=(sA, q2a[3]))
            snapcpy("s3r", bc_s3[:, :])
            snapcpy("r3r", bc_r3[:, :], wait=(sBC, gr3))

        yev = None
        for i2 in range(8):
            if i2 < 2:
                wslab = ([(sWQG, wq["w2_last"])] if j == 0
                         else [(sP, u2b[j - 1]["mm2_done"])])
            else:
                wslab = [(sP, st[f"mm2_{i2 - 2}"])]
            sd = dma("scalar" if i2 % 2 == 0 else "sync", sW2S[i2 % 2],
                     w2s_p[i2 % 2][:, :], w2q_dram[i2], wait=wslab)
            bk2 = 2 + (i2 % 2)
            pv = None
            for kk in range(32):
                w_ = []
                if kk == 0:
                    w_.append((sW2S[i2 % 2], sd))
                if i2 == 0 and kk % 8 == 0:
                    w_.append((sA, q2a[kk // 8]))
                pv = pe_pb(bk2, lambda t, i2=i2, kk=kk, bk2=bk2: t.matmul(
                    pb[bk2][:, 0:SC], w2s_p[i2 % 2][:, 128 * kk:128 * (kk + 1)],
                    xq2_sb[kk][:, :], start=(kk == 0), stop=(kk == 31)),
                    wait=w_ or None)
            st[f"mm2_{i2}"] = pv

            def fy(v, i2=i2, p=p, bk2=bk2):
                v.tensor_tensor(y_sb[i2][:, :], pb[bk2][:, 0:SC], bc_r3[:, :],
                                Alu.mult)
                return v.tensor_tensor(y_sb[i2][:, :], y_sb[i2][:, :],
                                       x1_p[p][i2][:, :], Alu.add)
            wy = [(sP, pv), (sBC, gr3)]
            if j >= 1:
                wy.append((sY, u2b[j - 1]["yd"][7]))
            yev = dv_pb(bk2, fy, wait=wy)
            if j == 0 and i2 == 0:
                snapcpy("y0", y_sb[0][:, :])
            st.setdefault("yd", [None] * 8)
            st["yd"][i2] = dma("sync", sY,
                               y_out[128 * i2:128 * (i2 + 1), SC * j:SC * (j + 1)],
                               y_sb[i2][:, :], wait=[(sV, yev)])
        st["mm2_done"] = st["mm2_7"]
        st["yev_last"] = yev

    tap("halo", [(xqh_l[i][:, :], BF16, (128, 1), (sV, vHM)) for i in range(8)]
         + [(xqh_r[i][:, :], BF16, (128, 1), (sV, vHM))
            for i in range(8)]
         + [(r1h[0:1, :], F32, (1, 2), (sV, vHM))])
    emit_2a(0)
    tap("outs0", [(outs_sb[k][:, :], BF16, (128, SC), (sOUT, u2a[0]["od"]))
                  for k in range(8)])
    tap("x1_0", [(x1_p[0][k][:, :], F32, (128, SC), (sV, u2a[0]["x1_done"]))
                 for k in range(8)])
    tap("xq_0", [(xq_p[0][k][:, :], BF16, (128, SC + 2), (sV, u2a[0]["xq_done"]))
                 for k in range(8)])
    tap("rows0", [(r_C[0:1, :], F32, (1, SC), (sV, u2a[0]["xq_done"])),
                  (r_D[0:1, :], F32, (1, SC), (sV, u2a[0]["xq_done"])),
                  (r_r1[0][0:1, :], F32, (1, SC + 2), (sV, u2a[0]["xq_done"]))])
    for j in range(1, NSC):
        emit_2a(j)
        emit_2b(j - 1)
        if j == 1 and _TAP["name"] in ("g0", "xq2_0", "y0"):
            tap("g0", [(g_sb[i][:, :], F32, (128, SC), (sV, u2b[0]["q2_last"]))
                       for i in range(4)])
            tap("xq2_0", [(xq2_sb[i][:, :], BF16, (128, SC), (sV, u2b[0]["q2_last"]))
                          for i in range(4)])
            tap("y0", [(y_sb[k][:, :], F32, (128, SC), (sV, u2b[0]["yev_last"]))
                       for k in range(8)])
    emit_2b(NSC - 1)

    snap_dump()
    wait("sync", sY, CNT["sY"])
    wait("gpsimd", sY, CNT["sY"])
    wait("vector", sY, CNT["sY"])

    with nc.Block() as block:
        @block.tensor
        def _(t):
            for f in STREAM["tensor"]:
                f(t)

        @block.scalar
        def _(a):
            for f in STREAM["scalar"]:
                f(a)

        @block.vector
        def _(v):
            vp = _DrainProxy(v)
            for f in STREAM["vector"]:
                f(vp)

        @block.gpsimd
        def _(g):
            for f in STREAM["gpsimd"]:
                f(g)

        @block.sync
        def _(s):
            for f in STREAM["sync"]:
                f(s)

    for g in reversed(keep):
        g.__exit__(None, None, None)
    return nc


_DVE_OPS = frozenset((
    "tensor_tensor", "tensor_scalar", "tensor_single_scalar", "tensor_copy",
    "memset", "memzero", "reduce_sum", "reduce_max", "tensor_reduce",
    "reciprocal", "scalar_tensor_tensor", "select", "transpose",
    "tensor_add", "tensor_sub", "tensor_mul", "tensor_max", "tensor_relu",
    "tensor_scalar_add", "tensor_scalar_sub", "tensor_scalar_mul",
    "tensor_scalar_max", "tensor_scalar_min", "copy_predicated",
))


class _DrainProxy:

    def __init__(self, h):
        object.__setattr__(self, "_h", h)
        object.__setattr__(self, "_dirty", set())

    @staticmethod
    def _name(a):
        t = getattr(a, "tensor", None)
        return getattr(t, "name", None) if t is not None else None

    def __getattr__(self, nm):
        h = object.__getattribute__(self, "_h")
        dirty = object.__getattribute__(self, "_dirty")
        a = getattr(h, nm)
        if nm == "drain":
            def wd(*args, **kw):
                dirty.clear()
                return a(*args, **kw)
            return wd
        if nm in _DVE_OPS:
            def w(*args, **kw):
                names = [self._name(x) for x in list(args) + list(kw.values())]
                names = [x for x in names if x]
                outn = names[0] if names else None
                if any(x in dirty for x in names[1:]):
                    h.drain()
                    dirty.clear()
                if outn:
                    dirty.add(outn)
                return a(*args, **kw)
            return w
        return a


_NC_CACHE = {}


def _get_nc():
    if "nc" not in _NC_CACHE:
        _NC_CACHE["nc"] = build_program()
    return _NC_CACHE["nc"]


def make_in_maps(L0_attn_tokens, L0_attn_norm_w, L0_attn_a, L0_attn_b, L0_attn_c,
                 L0_attn_skip, L0_attn_w_out, L0_attn_b_out,
                 L0_ffn_w1, L0_ffn_conv_w, L0_ffn_conv_b, L0_ffn_w2):
    f32 = np.float32
    X = np.ascontiguousarray(np.asarray(L0_attn_tokens, f32).reshape(NT, D))
    XT = np.ascontiguousarray(X.T)
    a = np.asarray(L0_attn_a, f32)
    b = np.asarray(L0_attn_b, f32)
    c = np.asarray(L0_attn_c, f32)
    skip = np.asarray(L0_attn_skip, f32)
    nw = np.asarray(L0_attn_norm_w, f32)
    w_out = np.asarray(L0_attn_w_out, f32)
    b_out = np.asarray(L0_attn_b_out, f32)
    cw = np.asarray(L0_ffn_conv_w, f32).reshape(HID, 3)
    cb = np.asarray(L0_ffn_conv_b, f32).reshape(HID, 1)
    w_outT = np.ascontiguousarray(w_out.T)
    w1T = np.ascontiguousarray(np.asarray(L0_ffn_w1, f32).T)
    w2T = np.ascontiguousarray(np.asarray(L0_ffn_w2, f32).T)
    ident = np.eye(128, dtype=f32)
    onesc = np.ones((128, 1), f32)

    def blkdiag(m0, m1):
        z = np.zeros((128, 128), f32)
        z[0:64, 0:64] = m0
        z[64:128, 64:128] = m1
        return z

    in_maps = []
    for cid in range(NCORE):
        h0, h1 = 2 * cid, 2 * cid + 1
        hm = np.ones((128, 2), f32)
        if cid in (0, 4):
            hm[:, 0] = 0.0
        if cid in (3, 7):
            hm[:, 1] = 0.0
        rows = slice(RT * cid, RT * (cid + 1))
        in_maps.append({
            "x_tok": np.ascontiguousarray(X[rows]),
            "x_feat": np.ascontiguousarray(XT[128 * cid:128 * (cid + 1), :]),
            "x_ftok": np.ascontiguousarray(XT[:, rows]),
            "x_edge": np.ascontiguousarray(XT[:, [RT * cid, RT * (cid + 1) - 1]]),
            "a_blk": blkdiag(a[h0], a[h1]),
            "b_blk": blkdiag(b[h0], b[h1]),
            "c_blk": blkdiag(c[h0], c[h1]),
            "skip2": np.ascontiguousarray(np.concatenate([skip[h0], skip[h1]])[:, None]),
            "nw_sl": np.ascontiguousarray(nw[128 * cid:128 * (cid + 1)][:, None]),
            "w_outT": w_outT,
            "b_out_c": np.ascontiguousarray(b_out[:, None]),
            "w1T_in": w1T,
            "w2T_in": w2T,
            "cw_in": cw,
            "cb_in": cb,
            "ident_in": ident,
            "onesc_in": onesc,
            "hmask_in": hm,
            "mgc_in": np.concatenate([np.full((128, 1), MAGIC, f32),
                                      np.full((128, 1), -MAGIC, f32)], axis=1),
        })

    return in_maps


def kernel(**inputs):
    in_maps = make_in_maps(**inputs)
    nc = _get_nc()
    with allow_ici_all_to_all():
        res = run_bass_kernel_spmd(nc, in_maps, core_ids=list(range(NCORE)))
    Y = np.empty((D, NT), np.float32)
    for cid in range(NCORE):
        Y[:, RT * cid:RT * (cid + 1)] = res.results[cid]["y_out"]
    return np.ascontiguousarray(Y.T).reshape(2, T, D)


# version 17
# speedup vs baseline: 1.0287x; 1.0287x over previous
import numpy as np
import ml_dtypes

import concourse.bass as bass
import concourse.mybir as mybir
from concourse.bass_utils import run_bass_kernel_spmd
from concourse.aot_env import allow_ici_all_to_all

BFNP = ml_dtypes.bfloat16
F32 = mybir.dt.float32
BF16 = mybir.dt.bfloat16
Alu = mybir.AluOpType
Act = mybir.ActivationFunctionType
AXX = mybir.AxisListType.X

NCORE = 8
D = 1024
T = 16384
NT = 2 * T
RT = NT // NCORE
CK = 1024
NCH = T // CK
HID = 4096
SC = 256
NSC = RT // SC
MAGIC = float(np.float32(1.5 * 2 ** 23))
_TAP = {"name": None}


def build_program():
    nc = bass.Bass("TRN2", target_bir_lowering=False, debug=False)

    def din(n, s):
        return nc.dram_tensor(n, s, F32, kind="ExternalInput").ap()

    x_tok = din("x_tok", [RT, D])
    x_feat = din("x_feat", [128, NT])
    x_ftok = din("x_ftok", [D, RT])
    x_edge = din("x_edge", [D, 2])
    a_blk = din("a_blk", [128, 128])
    b_blk = din("b_blk", [128, 128])
    c_blk = din("c_blk", [128, 128])
    skip2 = din("skip2", [128, 1])
    nw_sl = din("nw_sl", [128, 1])
    w_outT = din("w_outT", [D, D])
    b_out_c = din("b_out_c", [D, 1])
    w1T_in = din("w1T_in", [D, HID])
    w2T_in = din("w2T_in", [HID, D])
    cw_in = din("cw_in", [HID, 3])
    cb_in = din("cb_in", [HID, 1])
    ident_in = din("ident_in", [128, 128])
    onesc_in = din("onesc_in", [128, 1])
    hmask_in = din("hmask_in", [128, 2])
    mgc_in = din("mgc_in", [128, 2])
    y_out = nc.dram_tensor("y_out", [D, RT], F32, kind="ExternalOutput").ap()

    ag_st_in = nc.dram_tensor("ag_st_in", [3, 32, 128], F32).ap()
    ag_st_out = nc.dram_tensor("ag_st_out", [NCORE, 3, 32, 128], F32).ap()
    a2a_in = nc.dram_tensor("a2a_in", [NCORE, 128, RT], BF16).ap()
    a2a_out = nc.dram_tensor("a2a_out", [NCORE, 128, RT], BF16).ap()
    egA_in = nc.dram_tensor("egA_in", [D, 2], BF16).ap()
    egA_out = nc.dram_tensor("egA_out", [NCORE, D, 2], BF16).ap()
    egB_in = nc.dram_tensor("egB_in", [1, 2], F32).ap()
    egB_out = nc.dram_tensor("egB_out", [NCORE, 1, 2], F32).ap()
    w2q_dram = nc.dram_tensor("w2q_dram", [NCORE, 128, HID], BF16).ap()
    w1q_dram = nc.dram_tensor("w1q_dram", [32, 128, D], BF16).ap()
    rowb = nc.dram_tensor("rowb", [10, 520], F32).ap()

    keep = []

    def sbuf(name, shape, dtype):
        g = nc.sbuf_tensor(name, shape, dtype)
        t = g.__enter__()
        keep.append(g)
        return t

    ARENA = 156 * 1024
    arena_guard = nc.sbuf_tensor("arena", [128, ARENA], mybir.dt.uint8)
    arena_t = arena_guard.__enter__()
    keep.append(arena_guard)
    arena_base = nc.lookup_mloc(arena_t).addr
    arena_off = {"S": 0, "F": 0}

    def asb(grp, name, shape, dtype):
        per_part = int(np.prod(shape[1:])) * mybir.dt.size(dtype)
        off = arena_off[grp]
        arena_off[grp] = (off + per_part + 31) // 32 * 32
        assert arena_off[grp] <= ARENA, f"arena {grp} overflow at {name}"
        h = nc.virtual_sbuf_tensor(name, list(shape), dtype)
        nc._tensor(h.name, h.shape, h.dtype, type="SB")
        mloc = nc.lookup_mloc(h)
        mloc.allocated = True
        mloc.addr = arena_base + off
        return h

    def psum_(name, shape):
        g = nc.psum_tensor(name, list(shape), F32)
        t = g.__enter__()
        keep.append(g)
        return t

    def mksem(name):
        g = nc.semaphore(name)
        s = g.__enter__()
        keep.append(g)
        return s

    STREAM = {e: [] for e in ("tensor", "scalar", "vector", "gpsimd", "sync")}
    CNT = {}

    class Sem:
        def __init__(self, name):
            self.h = mksem(name)
            self.name = name
            CNT[name] = 0

        def bump(self, by):
            CNT[self.name] += by
            return CNT[self.name]

        @property
        def v(self):
            return CNT[self.name]

    def emit(eng, fn, wait=None, inc=None, inc_by=1, fuse=False):
        waits = [] if wait is None else (wait if isinstance(wait, list) else [wait])
        waits = [(s, v) for (s, v) in waits if v is not None and v > 0]
        val = inc.bump(inc_by) if inc is not None else None
        ih = inc.h if inc is not None else None

        def build(h):
            if fn is None:
                for (s, v) in waits:
                    h.wait_ge(s.h, v)
                return
            if fuse and len(waits) == 1:
                i = fn(h)
                i._wait_ge(waits[0][0].h, waits[0][1])
            else:
                for (s, v) in waits:
                    h.wait_ge(s.h, v)
                i = fn(h)
            if ih is not None:
                if eng == "vector":
                    h.drain().then_inc(ih, inc_by)
                else:
                    i.then_inc(ih, inc_by)
        STREAM[eng].append(build)
        return val

    def wait(eng, sem, val):
        emit(eng, None, wait=(sem, val))

    def dma(eng, sem, out, in_, wait=None, slow=False):
        if slow:
            return emit(eng, lambda h: h.dma_start(out=out, in_=in_,
                                                   allow_slow_non_contiguous=True),
                        wait=wait, inc=sem, inc_by=16)
        return emit(eng, lambda h: h.dma_start(out=out, in_=in_), wait=wait,
                    inc=sem, inc_by=16)

    rowb_last = [0] * 10

    def repl(sem, out, row_tensor, offset, n, wait=None, slot=None):
        th = getattr(row_tensor, "tensor", row_tensor)
        if type(th).__name__ == "DRamTensorHandle":
            srcap = bass.AP(th, offset, [[0, 128], [1, n]])
            return emit("gpsimd", lambda h: h.dma_start(out=out, in_=srcap),
                        wait=wait, inc=sem, inc_by=16)
        assert slot is not None
        w_ = [] if wait is None else (wait if isinstance(wait, list) else [wait])
        if rowb_last[slot]:
            w_.append((sem, rowb_last[slot]))
        assert offset == 0
        row_ap = row_tensor[0:1, 0:n]
        v1 = emit("gpsimd", lambda h: h.dma_start(out=rowb[slot:slot + 1, 0:n],
                                                  in_=row_ap),
                  wait=w_ or None, inc=sBW, inc_by=16)
        srcap = bass.AP(rowb.tensor, slot * 520, [[0, 128], [1, n]])
        v2 = emit("gpsimd", lambda h: h.dma_start(out=out, in_=srcap),
                  wait=[(sBW, v1)], inc=sem, inc_by=16)
        rowb_last[slot] = v2
        return v2

    dbg_state = {"n": 0}
    snap_bufs = []

    def snap_alloc(shape, dtype):
        t = sbuf(f"snapb{len(snap_bufs)}", list(shape), dtype)
        snap_bufs.append((t, shape, dtype))
        return t

    SNAP = _TAP["name"] == "snap"
    snaps = {}
    if SNAP:
        snaps = {
            "w1s": snap_alloc([128, D], BF16),
            "xqp": snap_alloc([128, SC + 2], BF16),
            "h0": snap_alloc([128, SC + 2], F32),
            "g0": snap_alloc([128, SC], F32),
            "xq20": snap_alloc([128, SC], BF16),
            "s3r": snap_alloc([128, SC], F32),
            "r3r": snap_alloc([128, SC], F32),
            "y0": snap_alloc([128, SC], F32),
        }

    def snapcpy(name, srcap, wait=None):
        if not SNAP:
            return
        emit("vector", lambda v: v.tensor_copy(snaps[name][:, :], srcap), wait=wait,
             inc=sV)

    def snap_dump():
        if not SNAP:
            return
        for i, nm in enumerate(snaps):
            t, shape, dtype = snap_bufs[i][0], snap_bufs[i][1], snap_bufs[i][2]
            dout = nc.dram_tensor(f"dbg{i}", list(shape), dtype,
                                  kind="ExternalOutput").ap()
            dma("gpsimd", sG, dout[:, :], t[:, :], wait=[(sY, CNT["sY"])])

    def tap(name, parts):
        if _TAP["name"] != name:
            return
        for (ap, dtype, shape, wait_) in parts:
            dn = f"dbg{dbg_state['n']}"
            dbg_state['n'] += 1
            dout = nc.dram_tensor(dn, list(shape), dtype, kind="ExternalOutput").ap()
            emit("gpsimd", lambda h, dout=dout, ap=ap: h.dma_start(out=dout[:, :],
                                                                   in_=ap),
                 wait=wait_, inc=sG, inc_by=16)

    sSET = Sem("sSET")
    sWO = [Sem("sWO0"), Sem("sWO1")]
    sXT = [Sem("sXT0"), Sem("sXT1")]
    sWQ = Sem("sWQ")
    sXF = [Sem("sXF0"), Sem("sXF1")]
    sEO = Sem("sEO")
    sOUT = Sem("sOUT")
    sXF2 = [Sem("sXF2_0"), Sem("sXF2_1")]
    sW1S = [Sem("sW1S0"), Sem("sW1S1")]
    sW2S = [Sem("sW2S0"), Sem("sW2S1")]
    sST = Sem("sST")
    sRP = Sem("sRP")
    sA2A = [Sem("sA2A0"), Sem("sA2A1")]
    sWQG = Sem("sWQG")
    sEM = Sem("sEM")
    sEGW = Sem("sEGW")
    sHL = Sem("sHL")
    sBW = Sem("sBW")
    sBC = Sem("sBC")
    sG = Sem("sG")
    sV = Sem("sV")
    sP = Sem("sP")
    sA = Sem("sA")
    sC = Sem("sC")
    sM0 = Sem("sM0")
    sM1 = Sem("sM1")
    sT0 = Sem("sT0")
    sT1 = Sem("sT1")
    sY = Sem("sY")
    sMb = {0: sM0, 1: sM1}
    sTb = {0: sT0, 1: sT1}

    pb_sc = [psum_(f"pbsc{i}", (128, 512)) for i in range(4)]
    pb = [psum_(f"pb{i}", (128, 512)) for i in range(4)]
    pbr = {0: 0, 1: 0, 2: 0, 3: 0, 4: 0, 5: 0, 6: 0, 7: 0}

    def pe_pb(x, fn, wait=None, tag=None):
        w = [] if wait is None else (wait if isinstance(wait, list) else [wait])
        if pbr[x]:
            w.append((sV, pbr[x]))
        return emit("tensor", fn, wait=w or None, inc=sP)

    def dv_pb(x, fn, wait=None):
        v = emit("vector", fn, wait=wait, inc=sV)
        pbr[x] = v
        return v

    wA = sbuf("wA", [128, 128], BF16)
    wB = sbuf("wB", [128, 128], BF16)
    wC = sbuf("wC", [128, 128], BF16)
    skip_s = sbuf("skip_s", [128, 1], F32)
    nw_s = sbuf("nw_s", [128, 1], F32)
    ident_s = sbuf("ident_s", [128, 128], F32)
    onesc_s = sbuf("onesc_s", [128, 1], F32)
    hmask_s = sbuf("hmask_s", [128, 2], F32)
    bout_s = sbuf("bout_s", [128, 8], F32)
    cw_s = sbuf("cw_s", [128, 96], F32)
    cb_s = sbuf("cb_s", [128, 32], F32)
    sw1_c = sbuf("sw1_c", [128, 1], F32)
    sw2_c = sbuf("sw2_c", [128, 1], F32)
    sw1_row = sbuf("sw1_row", [1, 1], F32)
    sw2_row = sbuf("sw2_row", [1, 1], F32)
    mgc_s = sbuf("mgc_s", [128, 2], F32)
    woutb = [sbuf(f"woutb{k}", [128, D], BF16) for k in range(8)]
    xqh_l = [sbuf(f"xqhl{k}", [128, 1], BF16) for k in range(8)]
    xqh_r = [sbuf(f"xqhr{k}", [128, 1], BF16) for k in range(8)]
    r1h = sbuf("r1h", [1, 2], F32)

    stage = asb("S", "stage", [128, 384], F32)
    xf_p = [asb("S", f"xf{p}", [128, 2 * CK], F32) for p in range(3)]
    xr_p = [asb("S", f"xr{p}", [128, 2 * CK], BF16) for p in range(3)]
    u_p = [asb("S", f"u{p}", [128, 2 * CK], F32) for p in range(2)]
    z_p = [asb("S", f"z{p}", [128, 2 * (CK + 1)], BF16) for p in range(2)]
    mx_p = [asb("S", f"mxp{p}", [128, 2 * CK], BF16) for p in range(2)]
    mbc = asb("S", "mbc", [128, CK], F32)
    sbc = asb("S", "sbc", [128, CK], F32)
    xt_p = [asb("S", f"xt{p}", [128, D], F32) for p in range(2)]
    sq_s = asb("S", "sq_s", [128, D], F32)
    st_m = asb("S", "st_m", [128, 32], F32)
    st_l = asb("S", "st_l", [128, 32], F32)
    st_s = asb("S", "st_s", [128, 32], F32)
    st_r = asb("S", "st_r", [128, 32], F32)
    st_q = asb("S", "st_q", [128, 32], F32)
    st_row = asb("S", "st_row", [32, 128], F32)
    c_t1 = asb("S", "c_t1", [128, 1], F32)
    c_t2 = asb("S", "c_t2", [128, 1], F32)
    c_t3 = asb("S", "c_t3", [128, 1], F32)
    wq_ld = [asb("S", f"wqld{p}", [128, HID], F32) for p in range(2)]
    wq_c1 = asb("S", "wq_c1", [128, 8], F32)
    wq_c2 = asb("S", "wq_c2", [128, 32], F32)
    w2tile = asb("S", "w2tile", [128, D], BF16)
    w1tile = asb("S", "w1tile", [128, HID], BF16)
    e_outs = [asb("S", f"eo{k}", [128, 2], BF16) for k in range(8)]
    e_x = [asb("S", f"ex{k}", [128, 2], F32) for k in range(8)]
    e_x1 = [asb("S", f"ex1{k}", [128, 2], F32) for k in range(8)]
    e_xq = [asb("S", f"exq{k}", [128, 2], BF16) for k in range(8)]
    e_m = asb("S", "e_m", [128, 2], F32)
    e_l = asb("S", "e_l", [128, 2], F32)
    e_sq = asb("S", "e_sq", [128, 2], F32)
    e_mx = asb("S", "e_mx", [128, 2], F32)
    e_mn = asb("S", "e_mn", [128, 2], F32)
    e_bC = asb("S", "e_bC", [128, 2], F32)
    e_bD = asb("S", "e_bD", [128, 2], F32)
    er_m = asb("S", "er_m", [1, 2], F32)
    er_i = asb("S", "er_i", [1, 2], F32)
    er_x = asb("S", "er_x", [1, 2], F32)
    er_n = asb("S", "er_n", [1, 2], F32)
    er_C = asb("S", "er_C", [1, 2], F32)
    er_D = asb("S", "er_D", [1, 2], F32)
    er_r1 = asb("S", "er_r1", [1, 2], F32)
    er_t = asb("S", "er_t", [1, 2], F32)

    outs_sb = [asb("F", f"outs{k}", [128, SC], BF16) for k in range(8)]
    xf2_p = [[asb("F", f"xf2_{p}_{k}", [128, SC], F32) for k in range(8)] for p in range(2)]
    x1_p = [[asb("F", f"x1_{p}_{k}", [128, SC], F32) for k in range(8)] for p in range(2)]
    xq_p = [[asb("F", f"xq_{p}_{k}", [128, SC + 2], BF16) for k in range(8)] for p in range(2)]
    g_sb = [asb("F", f"g_{i}", [128, SC], F32) for i in range(32)]
    h_pp = [asb("F", f"h_{p}", [128, SC + 2], F32) for p in range(4)]
    xq2_sb = [asb("F", f"xq2_{i}", [128, SC], BF16) for i in range(32)]
    y_sb = [asb("F", f"ysb{k}", [128, SC], F32) for k in range(8)]
    w2s_p = [asb("F", f"w2s{p}", [128, HID], BF16) for p in range(2)]
    w1s_p = [asb("F", f"w1s{p}", [128, D], BF16) for p in range(3)]
    bc_m = asb("F", "bc_m", [128, SC], F32)
    bc_l = asb("F", "bc_l", [128, SC], F32)
    bc_C = asb("F", "bc_C", [128, SC], F32)
    bc_D = asb("F", "bc_D", [128, SC], F32)
    bc_r1 = asb("F", "bc_r1", [128, SC + 2], F32)
    bc_s3 = asb("F", "bc_s3", [128, SC], F32)
    bc_r3 = asb("F", "bc_r3", [128, SC], F32)
    sq2 = asb("F", "sq2", [128, SC], F32)
    amx = [asb("F", f"amx{i}", [128, SC], F32) for i in range(8)]
    t_mx = asb("F", "t_mx", [128, SC], F32)
    t_mn = asb("F", "t_mn", [128, SC], F32)
    colB = asb("F", "colB", [128, 4], F32)
    r_m2 = asb("F", "r_m2", [1, SC], F32)
    r_i2 = asb("F", "r_i2", [1, SC], F32)
    r_mx = asb("F", "r_mx", [1, SC], F32)
    r_mn = asb("F", "r_mn", [1, SC], F32)
    r_C = asb("F", "r_C", [1, SC], F32)
    r_D = asb("F", "r_D", [1, SC], F32)
    r_r1 = [asb("F", f"r_r1{p}", [1, SC + 2], F32) for p in range(2)]
    r_s3 = asb("F", "r_s3", [1, SC], F32)
    r_r3 = asb("F", "r_r3", [1, SC], F32)
    r_t = asb("F", "r_t", [1, SC], F32)

    dma("sync", sSET, ident_s[:, :], ident_in[:, :])
    dma("sync", sSET, onesc_s[:, :], onesc_in[:, :])
    dma("sync", sSET, hmask_s[:, :], hmask_in[:, :])
    dma("sync", sSET, mgc_s[:, :], mgc_in[:, :])
    dma("sync", sSET, skip_s[:, :], skip2[:, :])
    dma("sync", sSET, nw_s[:, :], nw_sl[:, :])
    dma("sync", sSET, stage[:, 0:128], a_blk[:, :])
    dma("sync", sSET, stage[:, 128:256], b_blk[:, :])
    dma("sync", sSET, stage[:, 256:384], c_blk[:, :])
    for i in range(8):
        dma("sync", sSET, bout_s[:, i:i + 1], b_out_c[128 * i:128 * (i + 1), :])
    for i in range(32):
        dma("sync", sSET, cw_s[:, 3 * i:3 * i + 3], cw_in[128 * i:128 * (i + 1), :])
        dma("sync", sSET, cb_s[:, i:i + 1], cb_in[128 * i:128 * (i + 1), :])
    vABC = sSET.v

    emit("scalar", lambda a: a.activation(wA[:, :], stage[:, 0:128], Act.Tanh),
         wait=(sSET, vABC), inc=sA)
    emit("vector", lambda v: v.tensor_copy(wB[:, :], stage[:, 128:256]), wait=(sSET, vABC))
    emit("vector", lambda v: v.tensor_copy(wC[:, :], stage[:, 256:384]))
    emit("vector", lambda v: v.memset(z_p[0][:, :], 0.0))
    emit("vector", lambda v: v.memset(z_p[1][:, :], 0.0))

    wout_cast = []
    for k in range(8):
        w_ = [(sV, wout_cast[k - 2])] if k >= 2 else None
        vW = dma("sync", sWO[k % 2], xf_p[1][:, (k % 2) * D:(k % 2) * D + D],
                 w_outT[128 * k:128 * (k + 1), :], wait=w_)
        wout_cast.append(emit("vector", lambda v, k=k: v.tensor_copy(
            woutb[k][:, :], xf_p[1][:, (k % 2) * D:(k % 2) * D + D]),
            wait=(sWO[k % 2], vW), inc=sV))
    WOUT_DONE = wout_cast[-1]

    NT0 = RT // 128
    tdone = []
    for it in range(NT0):
        w_ = [(sV, tdone[it - 2])] if it >= 2 else None
        vX = dma("sync", sXT[it % 2], xt_p[it % 2][:, :],
                 x_tok[128 * it:128 * (it + 1), :], wait=w_)

        def fB(v, it=it, vX=vX):
            xt = xt_p[it % 2]
            v.reduce_sum(c_t1[:, :], xt[:, :], axis=AXX)._wait_ge(sXT[it % 2].h, vX)
            v.tensor_tensor(sq_s[:, :], xt[:, :], xt[:, :], Alu.mult)
            v.reduce_sum(c_t2[:, :], sq_s[:, :], axis=AXX)
            v.tensor_scalar(st_m[:, it:it + 1], c_t1[:, :], 1.0 / D, None, Alu.mult)
            v.tensor_scalar(c_t2[:, :], c_t2[:, :], 1.0 / D, None, Alu.mult)
            v.tensor_tensor(c_t3[:, :], st_m[:, it:it + 1], st_m[:, it:it + 1], Alu.mult)
            v.tensor_tensor(c_t2[:, :], c_t2[:, :], c_t3[:, :], Alu.subtract)
            v.tensor_scalar(c_t3[:, :], c_t2[:, :], 1e-5, None, Alu.add)
            v.reciprocal(st_r[:, it:it + 1], c_t3[:, :])
            v.tensor_tensor(c_t2[:, :], c_t2[:, :], st_r[:, it:it + 1], Alu.mult)
            v.tensor_scalar(c_t2[:, :], c_t2[:, :], 1e-4, None, Alu.add)
            return v.reciprocal(st_q[:, it:it + 1], c_t2[:, :])
        tdone.append(emit("vector", fB, inc=sV))
    aLN = emit("scalar", lambda a: a.activation(st_l[:, :], st_r[:, :], Act.Sqrt),
               wait=(sV, tdone[-1]), inc=sA)
    aRM = emit("scalar", lambda a: a.activation(st_q[:, :], st_q[:, :], Act.Sqrt), inc=sA)
    vSK = emit("vector", lambda v: v.tensor_tensor(st_s[:, :], st_l[:, :], st_q[:, :],
                                                   Alu.mult), wait=(sA, aRM), inc=sV)
    tap("str", [(st_r[:, :], F32, (128, 32), (sV, vSK))])
    tap("st", [(st_m[:, :], F32, (128, 32), (sV, vSK)),
               (st_l[:, :], F32, (128, 32), (sV, vSK)),
               (st_s[:, :], F32, (128, 32), (sV, vSK))])
    gp_st = []
    for j, st in enumerate((st_m, st_l, st_s)):
        w_ = [(sV, vSK)] if j == 0 else [(sST, gp_st[-1])]
        pT = pe_pb(0, lambda t, st=st: t.transpose(pb[0][0:32, 0:128], st[:, :],
                                                   ident_s[:, :]), wait=w_)
        vT = dv_pb(0, lambda v: v.tensor_copy(st_row[:, :], pb[0][0:32, 0:128]),
                   wait=(sP, pT))
        gp_st.append(dma("gpsimd", sST, ag_st_in[j], st_row[:, :], wait=[(sV, vT)]))
    ccST = emit("gpsimd", lambda g: g.collective_compute(
        "AllGather", Alu.bypass, replica_groups=[list(range(NCORE))],
        ins=[ag_st_in.opt()], outs=[ag_st_out.opt()]),
        wait=(sST, gp_st[-1]), inc=sC)

    xr_done = [None] * (NCH + 2)
    xf_done = [None] * (NCH + 2)
    u_done = [None] * (NCH + 1)
    mxe_done = [None] * NCH
    a2a_dmas = [None] * NCH
    wq = {"lastv": 0, "sw_g": 0, "w2_last": 0, "w1_last": 0}
    N_DMA_SYNC = [0]

    def emit_xr(k):
        w_ = []
        if k >= 3:
            w_.append((sV, xr_done[k - 3]))
        if k == 1:
            w_.append((sV, WOUT_DONE))
        dma("sync", sXF[k % 2], xf_p[k % 3][:, 0:CK], x_feat[:, CK * k:CK * (k + 1)],
            wait=w_ or None)
        d2 = dma("sync", sXF[k % 2], xf_p[k % 3][:, CK:2 * CK],
                 x_feat[:, T + CK * k:T + CK * (k + 1)])
        xf_done[k] = d2
        last_xr = None
        for b in range(2):
            R0 = b * T + CK * k
            blk, off = R0 // RT, R0 % RT
            w2_ = []
            if k == 0 and b == 0:
                w2_.append((sC, ccST))
            if last_xr is not None:
                w2_.append((sV, last_xr))
            elif k >= 1:
                w2_.append((sV, xr_done[k - 1]))
            repl(sRP, mbc[:, :], ag_st_out.tensor, ((blk * 3 + 0) * 32 * 128) + off, CK,
                 wait=w2_ or None)
            gv = repl(sRP, sbc[:, :], ag_st_out.tensor, ((blk * 3 + 2) * 32 * 128) + off, CK)
            cols = slice(b * CK, b * CK + CK)

            def fxr(v, k=k, cols=cols):
                xf = xf_p[k % 3]
                v.tensor_tensor(xf[:, cols], xf[:, cols], mbc[:, :], Alu.subtract)
                v.tensor_tensor(xf[:, cols], xf[:, cols], sbc[:, :], Alu.mult)
                return v.tensor_scalar(xr_p[k % 3][:, cols], xf[:, cols], nw_s[:, 0:1],
                                       None, Alu.mult)
            last_xr = emit("vector", fxr, wait=[(sXF[k % 2], d2), (sRP, gv)], inc=sV)
        xr_done[k] = last_xr

    def emit_u(k):
        last = None
        for jb in range(4):
            w_ = []
            if jb == 0:
                w_.append((sV, xr_done[k]))
            pv = pe_pb(jb % 2, lambda t, jb=jb, k=k: t.matmul(
                pb[jb % 2][:, 0:512], wB[:, :],
                xr_p[k % 3][:, 512 * jb:512 * (jb + 1)], start=True, stop=True),
                wait=w_ or None)
            w2_ = [(sP, pv)]
            if jb == 0 and k >= 2:
                w2_ += [(sTb[0], CK * (k - 1)), (sTb[1], CK * (k - 1))]
            last = dv_pb(jb % 2, lambda v, jb=jb, k=k: v.tensor_copy(
                u_p[k % 2][:, 512 * jb:512 * (jb + 1)], pb[jb % 2][:, 0:512]),
                wait=w2_)
        u_done[k] = last

    def emit_scan(k):
        wait("tensor", sV, u_done[k])
        wait("scalar", sV, u_done[k])
        for j in range(1, CK + 1):
            tp = CK * k + j
            for b in range(2):
                if j == 1:
                    src = (z_p[(k - 1) % 2][:, b * (CK + 1) + CK:b * (CK + 1) + CK + 1]
                           if k > 0 else z_p[0][:, b * (CK + 1):b * (CK + 1) + 1])
                else:
                    src = z_p[k % 2][:, b * (CK + 1) + j - 1:b * (CK + 1) + j]
                dst = z_p[k % 2][:, b * (CK + 1) + j:b * (CK + 1) + j + 1]
                bias = u_p[k % 2][:, b * CK + j - 1:b * CK + j]
                bank = pb_sc[2 * b + (tp % 2)]
                emit("tensor", lambda t, bank=bank, src=src: t.matmul(
                    bank[:, 0:1], wA[:, :], src, start=True, stop=True),
                    wait=(sTb[b], tp - 1), inc=sMb[b], fuse=True)
                emit("scalar", lambda a, bank=bank, dst=dst, bias=bias: a.activation(
                    dst, bank[:, 0:1], Act.Tanh, bias=bias),
                    wait=(sMb[b], tp), inc=sTb[b], fuse=True)

    def emit_mixed(k):
        last = None
        for b in range(2):
            for jb in range(2):
                x = 2 + ((2 * b + jb) % 2)
                w_ = []
                if b == 0 and jb == 0:
                    w_ += [(sTb[0], CK * (k + 1)), (sTb[1], CK * (k + 1))]
                pv = pe_pb(x, lambda t, k=k, b=b, jb=jb: t.matmul(
                    pb[2 + ((2 * b + jb) % 2)][:, 0:512], wC[:, :],
                    z_p[k % 2][:, b * (CK + 1) + 1 + 512 * jb:
                               b * (CK + 1) + 1 + 512 * (jb + 1)],
                    start=True, stop=True), wait=w_ or None)
                w2_ = [(sP, pv)]
                if b == 0 and jb == 0 and k >= 2:
                    w2_.append((sA2A[k % 2], a2a_dmas[k - 2]))

                def fmx(v, k=k, b=b, jb=jb, x=x):
                    c0 = b * CK + 512 * jb
                    return v.scalar_tensor_tensor(
                        mx_p[k % 2][:, c0:c0 + 512], xr_p[k % 3][:, c0:c0 + 512],
                        skip_s[:, 0:1], pb[x][:, 0:512], Alu.mult, Alu.add)
                last = dv_pb(x, fmx, wait=w2_)
        mxe_done[k] = last
        for b in range(2):
            blk = (0 if b == 0 else 4) + k // 4
            off = CK * (k % 4)
            a2a_dmas[k] = dma("gpsimd", sA2A[k % 2], a2a_in[blk, :, off:off + CK],
                              mx_p[k % 2][:, b * CK:b * CK + CK],
                              wait=[(sV, mxe_done[k])] if b == 0 else None)

    def _rsabs(v, out, in_, semh, val):
        i = v.reduce_sum(out, in_, axis=AXX, apply_absolute_value=True)
        i._wait_ge(semh, val)
        return i

    def _wq_w1abs(kk):
        w_ = [(sV, wq["lastv"])] if wq["lastv"] else None
        vL = dma("sync", sWQ, wq_ld[kk % 2][:, :],
                 w1T_in[128 * kk:128 * (kk + 1), :], wait=w_)
        wq["lastv"] = emit("vector", lambda v, kk=kk, vL=vL: _rsabs(
            v, wq_c1[:, kk:kk + 1], wq_ld[kk % 2][:, :], sWQ.h, vL), inc=sV)

    def _wq_w2abs(kk):
        vL = dma("sync", sWQ, wq_ld[kk % 2][:, 0:D],
                 w2T_in[128 * kk:128 * (kk + 1), :], wait=[(sV, wq["lastv"])])
        wq["lastv"] = emit("vector", lambda v, kk=kk, vL=vL: _rsabs(
            v, wq_c2[:, kk:kk + 1], wq_ld[kk % 2][:, 0:D], sWQ.h, vL), inc=sV)

    def _wq_scales(_):
        p1 = pe_pb(1, lambda t: t.matmul(pb[1][0:1, 0:8], onesc_s[:, :],
                                         wq_c1[:, :], start=True, stop=True),
                   wait=(sV, wq["lastv"]))
        p2 = pe_pb(1, lambda t: t.matmul(pb[1][0:1, 8:40], onesc_s[:, :],
                                         wq_c2[:, :], start=True, stop=True))

        def fsw(v):
            v.reduce_sum(sw1_row[:, :], pb[1][0:1, 0:8], axis=AXX)
            v.tensor_scalar(sw1_row[:, :], sw1_row[:, :], 1.0 / (D * HID), None, Alu.mult)
            v.tensor_scalar(sw1_row[:, :], sw1_row[:, :], 1e-5, None, Alu.max)
            v.reciprocal(sw1_row[:, :], sw1_row[:, :])
            v.reduce_sum(sw2_row[:, :], pb[1][0:1, 8:40], axis=AXX)
            v.tensor_scalar(sw2_row[:, :], sw2_row[:, :], 1.0 / (D * HID), None, Alu.mult)
            v.tensor_scalar(sw2_row[:, :], sw2_row[:, :], 1e-5, None, Alu.max)
            return v.reciprocal(sw2_row[:, :], sw2_row[:, :])
        wq["lastv"] = dv_pb(1, fsw, wait=[(sP, p2)])
        repl(sBC, sw1_c[:, :], sw1_row, 0, 1, wait=[(sV, wq["lastv"])], slot=0)
        wq["sw_g"] = repl(sBC, sw2_c[:, :], sw2_row, 0, 1, slot=1)
        wait("vector", sBC, wq["sw_g"])

    def _wq_w1q(kk):
        vL = dma("sync", sWQ, wq_ld[kk % 2][:, :],
                 w1T_in[128 * kk:128 * (kk + 1), :], wait=[(sV, wq["lastv"])])
        w_ = [(sWQG, wq["w1_last"])] if wq["w1_last"] else None

        def fq1(v, kk=kk, vL=vL):
            w = wq_ld[kk % 2]
            v.tensor_scalar(w[:, :], w[:, :], sw1_c[:, 0:1], None,
                            Alu.mult)._wait_ge(sWQ.h, vL)
            v.tensor_scalar(w[:, :], w[:, :], MAGIC, -MAGIC, Alu.add, Alu.add)
            return v.tensor_scalar(w1tile[:, :], w[:, :], 1.0, -1.0, Alu.min, Alu.max)
        wq["lastv"] = emit("vector", fq1, wait=w_, inc=sV)
        dst = bass.AP(w1q_dram.tensor, 128 * kk,
                      [[D, 128], [128 * D, 32], [1, 128]])
        src_ = w1tile[:, :].rearrange("p (i q) -> p i q", i=32, q=128)
        wq["w1_last"] = dma("gpsimd", sWQG, dst, src_, wait=[(sV, wq["lastv"])])

    def _wq_w2q(kk):
        vL = dma("sync", sWQ, wq_ld[kk % 2][:, 0:D],
                 w2T_in[128 * kk:128 * (kk + 1), :], wait=[(sV, wq["lastv"])])
        w_ = [(sWQG, wq["w2_last"])] if wq["w2_last"] else None

        def fq2(v, kk=kk, vL=vL):
            w = wq_ld[kk % 2]
            v.tensor_scalar(w[:, 0:D], w[:, 0:D], sw2_c[:, 0:1], None,
                            Alu.mult)._wait_ge(sWQ.h, vL)
            v.tensor_scalar(w[:, 0:D], w[:, 0:D], MAGIC, -MAGIC, Alu.add, Alu.add)
            return v.tensor_scalar(w2tile[:, :], w[:, 0:D], 1.0, -1.0,
                                   Alu.min, Alu.max)
        wq["lastv"] = emit("vector", fq2, wait=w_, inc=sV)
        dst = bass.AP(w2q_dram.tensor, 128 * kk,
                      [[HID, 128], [128 * HID, 8], [1, 128]])
        src_ = w2tile[:, :].rearrange("p (i q) -> p i q", i=8, q=128)
        wq["w2_last"] = dma("gpsimd", sWQG, dst, src_, wait=[(sV, wq["lastv"])])

    wq_tasks = ([("w1abs", kk) for kk in range(8)]
                + [("w2abs", kk) for kk in range(32)]
                + [("scales", 0)]
                + [("w1q", kk) for kk in range(8)]
                + [("w2q", kk) for kk in range(32)])
    _WQ_FN = {"w1abs": _wq_w1abs, "w2abs": _wq_w2abs, "scales": _wq_scales,
              "w1q": _wq_w1q, "w2q": _wq_w2q}

    def emit_wq_slice(k):
        n = 6 if k < NCH - 1 else len(wq_tasks)
        for _ in range(n):
            if not wq_tasks:
                return
            kind, kk = wq_tasks.pop(0)
            _WQ_FN[kind](kk)

    emit_xr(0)
    emit_xr(1)
    emit_u(0)
    tap("wA", [(wA[:, :], BF16, (128, 128), (sA, 1))])
    tap("xr0", [(xr_p[0][:, :], BF16, (128, 2 * CK), (sV, xr_done[0]))])
    tap("u0", [(u_p[0][:, :], F32, (128, 2 * CK), (sV, u_done[0]))])
    for k in range(NCH):
        if k + 2 < NCH:
            emit_xr(k + 2)
        emit_scan(k)
        if k == 0:
            tap("z0", [(z_p[0][:, :], BF16, (128, 2 * (CK + 1)),
                        [(sT0, CK), (sT1, CK)])])
        if k + 1 < NCH:
            emit_u(k + 1)
        emit_mixed(k)
        if k == 0:
            tap("mx0", [(mx_p[0][:, :], BF16, (128, 2 * CK), (sV, mxe_done[0]))])
        emit_wq_slice(k)
    tap("sw", [(sw1_c[:, :], F32, (128, 1), (sBC, wq["sw_g"])),
               (sw2_c[:, :], F32, (128, 1), (sBC, wq["sw_g"]))])

    def _a2a(g):
        with allow_ici_all_to_all():
            return g.collective_compute("AllToAll", Alu.bypass,
                                        replica_groups=[list(range(NCORE))],
                                        ins=[a2a_in.opt()], outs=[a2a_out.opt()])
    ccA2A = emit("gpsimd", _a2a, wait=(sA2A[(NCH - 1) % 2], a2a_dmas[-1]), inc=sC)

    eo_last = None
    for kk in range(8):
        eo_last = dma("sync", sEO, e_outs[kk][:, 0:1], a2a_out[kk, :, 0:1],
                      wait=[(sC, ccA2A)] if kk == 0 else None, slow=True)
        eo_last = dma("sync", sEO, e_outs[kk][:, 1:2], a2a_out[kk, :, RT - 1:RT],
                      slow=True)
        eo_last = dma("sync", sEO, e_x[kk][:, :], x_edge[128 * kk:128 * (kk + 1), :])
    repl(sEM, e_m[:, 0:1], ag_st_in.tensor, 0, 1,
         wait=[(sA2A[(NCH - 1) % 2], a2a_dmas[-1])])
    repl(sEM, e_m[:, 1:2], ag_st_in.tensor, RT - 1, 1)
    repl(sEM, e_l[:, 0:1], ag_st_in.tensor, RT, 1)
    gml = repl(sEM, e_l[:, 1:2], ag_st_in.tensor, 2 * RT - 1, 1)

    ep = []
    for i in range(8):
        pv = None
        for kk in range(8):
            w_ = []
            if i == 0 and kk == 0:
                w_ = [(sEO, eo_last)]
            pv = pe_pb(i % 2, lambda t, i=i, kk=kk: t.matmul(
                pb[i % 2][:, 0:2], woutb[kk][:, 128 * i:128 * (i + 1)],
                e_outs[kk][:, :], start=(kk == 0), stop=(kk == 7)), wait=w_ or None)

        def fex(v, i=i):
            v.tensor_tensor(e_sq[:, :], e_x[i][:, :], e_m[:, :], Alu.subtract)
            v.tensor_tensor(e_sq[:, :], e_sq[:, :], e_l[:, :], Alu.mult)
            v.tensor_tensor(e_sq[:, :], e_sq[:, :], e_x[i][:, :], Alu.add)
            return v.scalar_tensor_tensor(e_x1[i][:, :], pb[i % 2][:, 0:2],
                                          bout_s[:, i:i + 1], e_sq[:, :], Alu.add, Alu.add)
        ep.append(dv_pb(i % 2, fex, wait=[(sEM, gml), (sP, pv)]))
    pst = None
    for i in range(8):
        pst = pe_pb(2, lambda t, i=i: t.matmul(
            pb[2][0:1, 0:2], onesc_s[:, :], e_x1[i][:, :], start=(i == 0), stop=(i == 7)),
            wait=[(sV, ep[-1])] if i == 0 else None)
    pst2 = None
    for i in range(8):
        vq = emit("vector", lambda v, i=i: v.tensor_tensor(
            e_sq[:, :], e_x1[i][:, :], e_x1[i][:, :], Alu.mult),
            wait=[(sP, pst2)] if pst2 else None, inc=sV)
        pst2 = pe_pb(3, lambda t, i=i: t.matmul(
            pb[3][0:1, 0:2], onesc_s[:, :], e_sq[:, :], start=(i == 0), stop=(i == 7)),
            wait=(sV, vq))

    def fmm(v):
        v.tensor_copy(e_mx[:, :], e_x1[0][:, :])
        v.tensor_copy(e_mn[:, :], e_x1[0][:, :])
        for i in range(1, 8):
            v.tensor_tensor(e_mx[:, :], e_mx[:, :], e_x1[i][:, :], Alu.max)
            v.tensor_tensor(e_mn[:, :], e_mn[:, :], e_x1[i][:, :], Alu.min)
        return v.engine_nop()
    vmm = emit("vector", fmm, inc=sV)
    last = vmm
    for (srcb, dstrow, op) in ((e_mx, er_x, Alu.max), (e_mn, er_n, Alu.min)):
        for e in range(2):
            pT = pe_pb(0, lambda t, srcb=srcb, e=e: t.transpose(
                pb[0][0:1, 0:128], srcb[:, e:e + 1], ident_s[:, :]), wait=[(sV, last)])
            last = dv_pb(0, lambda v, dstrow=dstrow, e=e, op=op: v.tensor_reduce(
                dstrow[:, e:e + 1], pb[0][0:1, 0:128], AXX, op), wait=(sP, pT))

    def fer(v):
        v.tensor_copy(er_m[:, :], pb[2][0:1, 0:2])
        v.tensor_scalar(er_m[:, :], er_m[:, :], 1.0 / D, None, Alu.mult)
        v.tensor_copy(er_i[:, :], pb[3][0:1, 0:2])
        v.tensor_scalar(er_i[:, :], er_i[:, :], 1.0 / D, None, Alu.mult)
        v.tensor_tensor(er_t[:, :], er_m[:, :], er_m[:, :], Alu.mult)
        v.tensor_tensor(er_i[:, :], er_i[:, :], er_t[:, :], Alu.subtract)
        v.tensor_scalar(er_i[:, :], er_i[:, :], 1e-5, None, Alu.add)
        return v.reciprocal(er_i[:, :], er_i[:, :])
    vr1 = dv_pb(2, fer, wait=[(sP, pst2), (sV, last)])
    pbr[3] = vr1
    ar1 = emit("scalar", lambda a: a.activation(er_i[:, :], er_i[:, :], Act.Sqrt),
               wait=(sV, vr1), inc=sA)

    def fer2(v):
        v.tensor_tensor(er_x[:, :], er_x[:, :], er_m[:, :], Alu.subtract)
        v.tensor_tensor(er_n[:, :], er_m[:, :], er_n[:, :], Alu.subtract)
        v.tensor_tensor(er_x[:, :], er_x[:, :], er_n[:, :], Alu.max)
        v.tensor_tensor(er_x[:, :], er_x[:, :], er_i[:, :], Alu.mult)
        v.tensor_scalar(er_x[:, :], er_x[:, :], 1e-5, None, Alu.max)
        v.reciprocal(er_x[:, :], er_x[:, :])
        v.tensor_scalar(er_x[:, :], er_x[:, :], 127.0, None, Alu.mult)
        v.tensor_tensor(er_C[:, :], er_i[:, :], er_x[:, :], Alu.mult)
        v.tensor_tensor(er_D[:, :], er_m[:, :], er_C[:, :], Alu.mult)
        v.tensor_scalar(er_D[:, :], er_D[:, :], -1.0, None, Alu.mult)
        v.tensor_scalar(er_r1[:, :], er_x[:, :], sw1_row[0:1, 0:1], None, Alu.mult)
        return v.reciprocal(er_r1[:, :], er_r1[:, :])
    vr2 = emit("vector", fer2, wait=(sA, ar1), inc=sV)
    repl(sBC, e_bC[:, :], er_C, 0, 2, wait=[(sV, vr2)], slot=2)
    gD = repl(sBC, e_bD[:, :], er_D, 0, 2, slot=3)
    exq_last = None
    for i in range(8):
        def fxq(v, i=i):
            i1 = v.tensor_tensor(e_sq[:, :], e_x1[i][:, :], e_bC[:, :], Alu.mult)
            i1._wait_ge(sBC.h, gD)
            v.tensor_tensor(e_sq[:, :], e_sq[:, :], e_bD[:, :], Alu.add)
            return v.tensor_scalar(e_xq[i][:, :], e_sq[:, :], MAGIC, -MAGIC,
                                   Alu.add, Alu.add)
        vq = emit("vector", fxq, inc=sV)
        exq_last = dma("gpsimd", sEGW, egA_in[128 * i:128 * (i + 1), :], e_xq[i][:, :],
                       wait=[(sV, vq)])
    gB = dma("gpsimd", sEGW, egB_in[:, :], er_r1[:, :], wait=[(sV, vr2)])
    ccEA = emit("gpsimd", lambda g: g.collective_compute(
        "AllGather", Alu.bypass, replica_groups=[list(range(NCORE))],
        ins=[egA_in.opt()], outs=[egA_out.opt()]), wait=(sEGW, gB), inc=sC)
    ccEB = emit("gpsimd", lambda g: g.collective_compute(
        "AllGather", Alu.bypass, replica_groups=[list(range(NCORE))],
        ins=[egB_in.opt()], outs=[egB_out.opt()]), wait=(sEGW, gB), inc=sC)

    def fhalo(g):
        g.wait_ge(sC.h, ccEB)
        with g.register("rl") as rl, g.register("rr") as rr:
            pid = g.partition_id()
            g.reg_mov(rl, pid)
            g.reg_add(rl, rl, 7)
            g.reg_mod(rl, rl, 8)
            g.reg_mul(rl, rl, D * 2)
            g.reg_add(rl, rl, 1)
            g.reg_mov(rr, pid)
            g.reg_add(rr, rr, 1)
            g.reg_mod(rr, rr, 8)
            g.reg_mul(rr, rr, D * 2)
            for kk in range(8):
                g.dma_start(out=xqh_l[kk][:, :],
                            in_=bass.AP(egA_out.tensor, rl, [[2, 128], [1, 1]]),
                            allow_slow_non_contiguous=True).then_inc(sHL.h, 16)
                g.reg_add(rl, rl, 256)
                g.dma_start(out=xqh_r[kk][:, :],
                            in_=bass.AP(egA_out.tensor, rr, [[2, 128], [1, 1]]),
                            allow_slow_non_contiguous=True).then_inc(sHL.h, 16)
                g.reg_add(rr, rr, 256)
            g.reg_mov(rl, pid)
            g.reg_add(rl, rl, 7)
            g.reg_mod(rl, rl, 8)
            g.reg_mul(rl, rl, 2)
            g.reg_add(rl, rl, 1)
            g.reg_mov(rr, pid)
            g.reg_add(rr, rr, 1)
            g.reg_mod(rr, rr, 8)
            g.reg_mul(rr, rr, 2)
            g.dma_start(out=r1h[:, 0:1],
                        in_=bass.AP(egB_out.tensor, rl, [[1, 1], [1, 1]])
                        ).then_inc(sHL.h, 16)
            return g.dma_start(out=r1h[:, 1:2],
                               in_=bass.AP(egB_out.tensor, rr, [[1, 1], [1, 1]]))
    emit("gpsimd", fhalo, inc=sHL, inc_by=16)
    sHL.bump(16 * 17)
    gH = sHL.v
    vHM = None
    for kk in range(8):
        def fm(v, kk=kk):
            i1 = v.tensor_tensor(xqh_l[kk][:, :], xqh_l[kk][:, :], hmask_s[:, 0:1],
                                 Alu.mult)
            if kk == 0:
                i1._wait_ge(sHL.h, gH)
            v.tensor_tensor(xqh_r[kk][:, :], xqh_r[kk][:, :], hmask_s[:, 1:2], Alu.mult)
            return v.engine_nop()
        vHM = emit("vector", fm, inc=sV)

    u2a = [dict() for _ in range(NSC)]
    u2b = [dict() for _ in range(NSC)]

    def tree_tr(bank, dst_max, dst_min, wait_first):
        srcs = [(t_mx, 0), (t_mx, 1), (t_mn, 0), (t_mn, 1)]
        pT = None
        for q, (srcb, half) in enumerate(srcs):
            pT = pe_pb(bank, lambda t, q=q, srcb=srcb, half=half: t.transpose(
                pb[bank][:, 128 * q:128 * (q + 1)],
                srcb[:, 128 * half:128 * (half + 1)], ident_s[:, :]),
                wait=wait_first if q == 0 else None)

        def fred(v, bank=bank):
            for q in range(4):
                v.tensor_reduce(colB[:, q:q + 1], pb[bank][:, 128 * q:128 * (q + 1)],
                                AXX, Alu.max if q < 2 else Alu.min)
            return v.engine_nop()
        rv = dv_pb(bank, fred, wait=(sP, pT))
        pT2 = None
        for q in range(4):
            pT2 = pe_pb(bank, lambda t, bank=bank, q=q: t.transpose(
                pb[bank][0:1, 128 * q:128 * (q + 1)], colB[:, q:q + 1],
                ident_s[:, :]), wait=[(sV, rv)] if q == 0 else None)

        def fcpy(v, bank=bank, dst_max=dst_max, dst_min=dst_min):
            v.tensor_copy(dst_max[:, 0:128], pb[bank][0:1, 0:128])
            v.tensor_copy(dst_max[:, 128:256], pb[bank][0:1, 128:256])
            v.tensor_copy(dst_min[:, 0:128], pb[bank][0:1, 256:384])
            v.tensor_copy(dst_min[:, 128:256], pb[bank][0:1, 384:512])
            return v.engine_nop()
        return dv_pb(bank, fcpy, wait=(sP, pT2))

    def emit_2a(j):
        p = j % 2
        st = u2a[j]
        od = None
        for kk in range(8):
            w_ = []
            if kk == 0:
                if j == 0:
                    w_ += [(sC, ccEB), (sV, vHM), (sV, wq["lastv"]),
                           (sWQG, wq["w2_last"]), (sWQG, wq["w1_last"])]
                else:
                    w_.append((sP, u2a[j - 1]["proj_done"]))
            od = dma("sync", sOUT, outs_sb[kk][:, :], a2a_out[kk, :, SC * j:SC * (j + 1)],
                     wait=w_ or None)
        st["od"] = od
        xd = None
        for kk in range(8):
            w_ = []
            if kk == 0 and j >= 2:
                w_.append((sV, u2b[j - 2]["fam"]))
            xd = dma("sync", sXF2[p], xf2_p[p][kk][:, :],
                     x_ftok[128 * kk:128 * (kk + 1), SC * j:SC * (j + 1)],
                     wait=w_ or None)
        w_ = [(sV, u2a[j - 1]["x1_done"])] if j >= 1 else None
        repl(sBC, bc_m[:, :], ag_st_in.tensor, SC * j, SC, wait=w_)
        gml2 = repl(sBC, bc_l[:, :], ag_st_in.tensor, RT + SC * j, SC)

        def fcmb(v):
            v.tensor_tensor(bc_m[:, :], bc_m[:, :], bc_l[:, :], Alu.mult)
            return v.tensor_scalar(bc_l[:, :], bc_l[:, :], 1.0, None, Alu.add)
        vcmb = emit("vector", fcmb, wait=[(sBC, gml2)], inc=sV)

        def fx1m(v, p=p):
            for i in range(8):
                v.tensor_tensor(xf2_p[p][i][:, :], xf2_p[p][i][:, :],
                                bc_l[:, :], Alu.mult)
            return v.engine_nop()
        emit("vector", fx1m, wait=[(sXF2[p], xd), (sV, vcmb)])

        def fx1s(v, p=p):
            for i in range(8):
                v.tensor_tensor(xf2_p[p][i][:, :], xf2_p[p][i][:, :],
                                bc_m[:, :], Alu.subtract)
            return v.engine_nop()
        emit("vector", fx1s)

        pvh = [None, None]
        x1d = None
        for half in range(2):
            for i in range(4 * half, 4 * half + 4):
                pv = None
                for kk in range(8):
                    w_ = []
                    if i == 0 and kk == 0:
                        w_.append((sOUT, od))
                    pv = pe_pb(i % 4, lambda t, i=i, kk=kk: t.matmul(
                        pb[i % 4][:, 0:SC], woutb[kk][:, 128 * i:128 * (i + 1)],
                        outs_sb[kk][:, :], start=(kk == 0), stop=(kk == 7)),
                        wait=w_ or None)
            pvh[half] = pv

            def fx1c(v, half=half, p=p):
                for i in range(4 * half, 4 * half + 4):
                    v.scalar_tensor_tensor(x1_p[p][i][:, :], pb[i % 4][:, 0:SC],
                                           bout_s[:, i:i + 1], xf2_p[p][i][:, :],
                                           Alu.add, Alu.add)
                return v.engine_nop()
            x1d = emit("vector", fx1c, wait=[(sP, pv)], inc=sV)
            for bk in range(4):
                pbr[bk] = x1d
        st["proj_done"] = pvh[1]
        st["x1_done"] = x1d

        for i in range(8):
            pe_pb(2, lambda t, i=i, p=p: t.matmul(
                pb[2][0:1, 0:SC], onesc_s[:, :], x1_p[p][i][:, :],
                start=(i == 0), stop=(i == 7)),
                wait=[(sV, x1d)] if i == 0 else None)
        pms = CNT["sP"]
        pmq = None
        for i in range(8):
            vq = emit("scalar", lambda a, i=i, p=p: a.activation(
                sq2[:, :], x1_p[p][i][:, :], Act.Square),
                wait=[(sP, pmq)] if pmq else [(sV, x1d)], inc=sA)
            pmq = pe_pb(3, lambda t, i=i: t.matmul(
                pb[3][0:1, 0:SC], onesc_s[:, :], sq2[:, :],
                start=(i == 0), stop=(i == 7)), wait=(sA, vq))

        def fmt(v, p=p):
            xs = x1_p[p]
            for q in range(4):
                v.tensor_tensor(amx[q][:, :], xs[2 * q][:, :], xs[2 * q + 1][:, :],
                                Alu.max)
                v.tensor_tensor(amx[4 + q][:, :], xs[2 * q][:, :],
                                xs[2 * q + 1][:, :], Alu.min)
            v.tensor_tensor(amx[0][:, :], amx[0][:, :], amx[1][:, :], Alu.max)
            v.tensor_tensor(amx[2][:, :], amx[2][:, :], amx[3][:, :], Alu.max)
            v.tensor_tensor(amx[4][:, :], amx[4][:, :], amx[5][:, :], Alu.min)
            v.tensor_tensor(amx[6][:, :], amx[6][:, :], amx[7][:, :], Alu.min)
            v.tensor_tensor(t_mx[:, :], amx[0][:, :], amx[2][:, :], Alu.max)
            v.tensor_tensor(t_mn[:, :], amx[4][:, :], amx[6][:, :], Alu.min)
            return v.engine_nop()
        last = emit("vector", fmt, inc=sV)
        last = tree_tr(0, r_mx, r_mn, [(sV, last)])

        def frow(v):
            v.tensor_copy(r_m2[:, :], pb[2][0:1, 0:SC])
            v.tensor_scalar(r_m2[:, :], r_m2[:, :], 1.0 / D, None, Alu.mult)
            v.tensor_copy(r_i2[:, :], pb[3][0:1, 0:SC])
            v.tensor_scalar(r_i2[:, :], r_i2[:, :], 1.0 / D, None, Alu.mult)
            v.tensor_tensor(r_t[:, :], r_m2[:, :], r_m2[:, :], Alu.mult)
            v.tensor_tensor(r_i2[:, :], r_i2[:, :], r_t[:, :], Alu.subtract)
            v.tensor_scalar(r_i2[:, :], r_i2[:, :], 1e-5, None, Alu.add)
            return v.reciprocal(r_i2[:, :], r_i2[:, :])
        vrw = dv_pb(2, frow, wait=[(sP, pmq), (sV, last)])
        pbr[3] = vrw
        arw = emit("scalar", lambda a: a.activation(r_i2[:, :], r_i2[:, :], Act.Sqrt),
                   wait=(sV, vrw), inc=sA)

        def frow2(v, p=p):
            v.tensor_tensor(r_mx[:, :], r_mx[:, :], r_m2[:, :], Alu.subtract)
            v.tensor_tensor(r_mn[:, :], r_m2[:, :], r_mn[:, :], Alu.subtract)
            v.tensor_tensor(r_mx[:, :], r_mx[:, :], r_mn[:, :], Alu.max)
            v.tensor_tensor(r_mx[:, :], r_mx[:, :], r_i2[:, :], Alu.mult)
            v.tensor_scalar(r_mx[:, :], r_mx[:, :], 1e-5, None, Alu.max)
            v.reciprocal(r_mx[:, :], r_mx[:, :])
            v.tensor_scalar(r_mx[:, :], r_mx[:, :], 127.0, None, Alu.mult)
            v.tensor_tensor(r_C[:, :], r_i2[:, :], r_mx[:, :], Alu.mult)
            v.tensor_tensor(r_D[:, :], r_m2[:, :], r_C[:, :], Alu.mult)
            v.tensor_scalar(r_D[:, :], r_D[:, :], -1.0, None, Alu.mult)
            v.tensor_scalar(r_t[:, :], r_mx[:, :], sw1_row[0:1, 0:1], None, Alu.mult)
            v.reciprocal(r_t[:, :], r_t[:, :])
            return v.tensor_copy(r_r1[p][:, 1:SC + 1], r_t[:, :])
        w_ = [(sA, arw)]
        if j >= 2:
            w_.append((sBC, u2b[j - 2]["gr1"]))
        vr2_ = emit("vector", frow2, wait=w_, inc=sV)
        w_ = [(sV, vr2_)]
        if j >= 1:
            w_.append((sV, u2a[j - 1]["xq_done"]))
        repl(sBC, bc_C[:, :], r_C, 0, SC, wait=w_, slot=2)
        gD_ = repl(sBC, bc_D[:, :], r_D, 0, SC, slot=3)

        w_ = [(sBC, gD_)]
        if j >= 2:
            w_.append((sP, u2b[j - 2]["mm1_done"]))

        def fqa(v, p=p):
            for i in range(8):
                v.tensor_tensor(xf2_p[p][i][:, :], x1_p[p][i][:, :],
                                bc_C[:, :], Alu.mult)
            return v.engine_nop()
        emit("vector", fqa, wait=w_)

        def fqb(v, p=p):
            for i in range(8):
                v.tensor_tensor(xf2_p[p][i][:, :], xf2_p[p][i][:, :],
                                bc_D[:, :], Alu.add)
            return v.engine_nop()
        emit("vector", fqb)

        def fqr(v, p=p):
            for i in range(8):
                v.tensor_scalar(xq_p[p][i][:, 1:SC + 1], xf2_p[p][i][:, :],
                                MAGIC, -MAGIC, Alu.add, Alu.add)
            return v.engine_nop()
        xql = emit("vector", fqr, inc=sV)
        st["xq_scr"] = xql

        def fhal(v, j=j, p=p):
            for i in range(8):
                if j == 0:
                    v.tensor_copy(xq_p[p][i][:, 0:1], xqh_l[i][:, :])
                else:
                    v.tensor_copy(xq_p[p][i][:, 0:1], xq_p[1 - p][i][:, SC:SC + 1])
            if j == 0:
                v.tensor_copy(r_r1[p][:, 0:1], r1h[:, 0:1])
            else:
                v.tensor_copy(r_r1[p][:, 0:1], r_r1[1 - p][:, SC:SC + 1])
            return v.engine_nop()
        w_ = [(sV, xql)]
        if j == 0:
            w_.append((sV, vHM))
        st["xq_done"] = emit("vector", fhal, wait=w_, inc=sV)

    def emit_2b(j):
        p = j % 2
        st = u2b[j]

        def frh(v, j=j, p=p):
            for i in range(8):
                if j == NSC - 1:
                    v.tensor_copy(xq_p[p][i][:, SC + 1:SC + 2], xqh_r[i][:, :])
                else:
                    v.tensor_copy(xq_p[p][i][:, SC + 1:SC + 2], xq_p[1 - p][i][:, 1:2])
            if j == NSC - 1:
                v.tensor_copy(r_r1[p][:, SC + 1:SC + 2], r1h[:, 1:2])
            else:
                v.tensor_copy(r_r1[p][:, SC + 1:SC + 2], r_r1[1 - p][:, 1:2])
            return v.engine_nop()
        jj = j if j == NSC - 1 else j + 1
        w_ = [(sV, u2a[jj]["xq_done"])]
        vrh = emit("vector", frh, wait=w_, inc=sV)
        w_ = [(sV, vrh)]
        if j >= 1:
            w_.append((sV, u2b[j - 1]["hev_last"]))
        gr1 = repl(sBC, bc_r1[:, :], r_r1[p], 0, SC + 2, wait=w_, slot=4)
        st["gr1"] = gr1

        hev = None
        sil = None
        sil_t = [None] * 32
        for i in range(32):
            bk = 4 + (i % 4)
            if i < 3:
                if j == 0:
                    wsl = [(sWQG, wq["w1_last"])] if i == 0 else None
                else:
                    wsl = [(sP, u2b[j - 1][f"mm1_{[30, 31, 29][i]}"])]
            else:
                wsl = [(sP, st[f"mm1_{i - 3}"])]
            sd1 = dma("gpsimd", sW1S[i % 2], w1s_p[i % 3][:, :], w1q_dram[i],
                      wait=wsl)
            pv = None
            for kk in range(8):
                w_ = []
                if kk == 0:
                    w_.append((sW1S[i % 2], sd1))
                    if i == 0:
                        w_.append((sV, vrh))
                def fmm1(t, i=i, kk=kk, p=p, bk=bk):
                    return t.matmul(
                        pb_sc[bk - 4][:, 0:SC + 2],
                        w1s_p[i % 3][:, 128 * kk:128 * (kk + 1)],
                        xq_p[p][kk][:, :], start=(kk == 0), stop=(kk == 7))
                wt = list(w_)
                if kk == 0 and pbr[bk]:
                    wt.append((sV, pbr[bk]))
                pv = emit("tensor", fmm1, wait=wt or None, inc=sP)
            st[f"mm1_{i}"] = pv
            if i % 4 == 3:
                i0 = i - 3

                def fgrp(v, i0=i0, pv=pv, gr1=gr1):
                    for g in range(4):
                        i1 = v.tensor_tensor(h_pp[g][:, :], pb_sc[g][:, 0:SC + 2],
                                             bc_r1[:, :], Alu.mult)
                        if g == 0:
                            i1._wait_ge(sBC.h, gr1)
                    for g in range(4):
                        v.tensor_scalar(g_sb[i0 + g][:, :], h_pp[g][:, 1:SC + 1],
                                        cw_s[:, 3 * (i0 + g) + 1:3 * (i0 + g) + 2],
                                        None, Alu.mult)
                    for g in range(4):
                        v.scalar_tensor_tensor(g_sb[i0 + g][:, :], h_pp[g][:, 0:SC],
                                               cw_s[:, 3 * (i0 + g):3 * (i0 + g) + 1],
                                               g_sb[i0 + g][:, :], Alu.mult, Alu.add)
                    for g in range(4):
                        v.scalar_tensor_tensor(g_sb[i0 + g][:, :], h_pp[g][:, 2:SC + 2],
                                               cw_s[:, 3 * (i0 + g) + 2:3 * (i0 + g) + 3],
                                               g_sb[i0 + g][:, :], Alu.mult, Alu.add)
                    return v.engine_nop()
                wg = [(sP, pv)]
                if i == 3 and j >= 1:
                    wg.append((sA, u2b[j - 1]["q2_act"]))
                hev = emit("vector", fgrp, wait=wg, inc=sV)
                for g in range(4):
                    pbr[4 + g] = hev
                if j == 0 and i0 == 0:
                    snapcpy("w1s", w1s_p[0][:, :])
                    snapcpy("xqp", xq_p[p][0][:, :])
                    snapcpy("h0", h_pp[0][:, :])
                for g in range(4):
                    sil = emit("scalar", lambda a, i2_=i0 + g: a.activation(
                        g_sb[i2_][:, :], g_sb[i2_][:, :], Act.Silu,
                        bias=cb_s[:, i2_:i2_ + 1]), wait=(sV, hev) if g == 0 else None,
                        inc=sA)
                    sil_t[i0 + g] = sil
                if j == 0 and i0 == 0:
                    snapcpy("g0", g_sb[0][:, :], wait=(sA, sil))
        st["mm1_done"] = CNT["sP"]
        st["hev_last"] = hev

        xmn = xf2_p[p]

        def famA(v, p=p):
            for c in range(8):
                v.tensor_tensor(amx[c][:, :], g_sb[c][:, :], g_sb[8 + c][:, :],
                                Alu.max)
                v.tensor_tensor(xmn[c][:, :], g_sb[c][:, :], g_sb[8 + c][:, :],
                                Alu.min)
            return v.engine_nop()
        emit("vector", famA, wait=(sA, sil_t[15]))

        def famB(v, p=p):
            for c in range(8):
                v.tensor_tensor(amx[c][:, :], amx[c][:, :], g_sb[16 + c][:, :],
                                Alu.max)
                v.tensor_tensor(xmn[c][:, :], xmn[c][:, :], g_sb[16 + c][:, :],
                                Alu.min)
            for c in range(8):
                v.tensor_tensor(amx[c][:, :], amx[c][:, :], g_sb[24 + c][:, :],
                                Alu.max)
                v.tensor_tensor(xmn[c][:, :], xmn[c][:, :], g_sb[24 + c][:, :],
                                Alu.min)
            for c in range(4):
                v.tensor_tensor(amx[2 * c][:, :], amx[2 * c][:, :],
                                amx[2 * c + 1][:, :], Alu.max)
                v.tensor_tensor(xmn[2 * c][:, :], xmn[2 * c][:, :],
                                xmn[2 * c + 1][:, :], Alu.min)
            for c in range(2):
                v.tensor_tensor(amx[4 * c][:, :], amx[4 * c][:, :],
                                amx[4 * c + 2][:, :], Alu.max)
                v.tensor_tensor(xmn[4 * c][:, :], xmn[4 * c][:, :],
                                xmn[4 * c + 2][:, :], Alu.min)
            v.tensor_tensor(t_mx[:, :], amx[0][:, :], amx[4][:, :], Alu.max)
            v.tensor_tensor(t_mn[:, :], xmn[0][:, :], xmn[4][:, :], Alu.min)
            return v.engine_nop()
        last = emit("vector", famB, wait=(sA, sil_t[31]), inc=sV)
        st["fam"] = last
        last = tree_tr(2, r_s3, r_t, [(sV, last)])

        def fs3(v):
            v.tensor_scalar(r_t[:, :], r_t[:, :], -1.0, None, Alu.mult)
            v.tensor_tensor(r_s3[:, :], r_s3[:, :], r_t[:, :], Alu.max)
            v.tensor_scalar(r_s3[:, :], r_s3[:, :], 1e-5, None, Alu.max)
            v.reciprocal(r_s3[:, :], r_s3[:, :])
            v.tensor_scalar(r_s3[:, :], r_s3[:, :], 127.0, None, Alu.mult)
            v.tensor_scalar(r_r3[:, :], r_s3[:, :], sw2_row[0:1, 0:1], None, Alu.mult)
            return v.reciprocal(r_r3[:, :], r_r3[:, :])
        vs3 = emit("vector", fs3, wait=(sV, last), inc=sV)
        w_ = [(sV, vs3)]
        if j >= 1:
            w_.append((sV, u2b[j - 1]["q2_last"]))
        gs3 = repl(sBC, bc_s3[:, :], r_s3, 0, SC, wait=w_, slot=5)
        w2_ = [(sV, u2b[j - 1]["yev_last"])] if j >= 1 else None
        gr3 = repl(sBC, bc_r3[:, :], r_r3, 0, SC, wait=w2_, slot=6)

        q2l = [None] * 4
        for bq in range(4):
            def fq2m(v, bq=bq):
                for i in range(8 * bq, 8 * bq + 8):
                    v.tensor_tensor(g_sb[i][:, :], g_sb[i][:, :], bc_s3[:, :],
                                    Alu.mult)
                return v.engine_nop()
            q2l[bq] = emit("vector", fq2m,
                           wait=[(sBC, gs3)] if bq == 0 else None, inc=sV)
        st["q2_last"] = q2l[3]

        q2a = [None] * 4
        for bq in range(4):
            w_ = [(sV, q2l[bq])]
            if bq == 0 and j >= 1:
                w_.append((sP, u2b[j - 1]["mm2_done"]))

            def fr2(a, bq=bq):
                for g in range(8 * bq, 8 * bq + 8):
                    a.activation(g_sb[g][:, :], g_sb[g][:, :],
                                 Act.Identity, bias=mgc_s[:, 0:1])
                for g in range(8 * bq, 8 * bq + 8):
                    i_ = a.activation(xq2_sb[g][:, :], g_sb[g][:, :],
                                      Act.Identity, bias=mgc_s[:, 1:2])
                return i_
            q2a[bq] = emit("scalar", fr2, wait=w_, inc=sA)
        st["q2_act"] = q2a[3]
        if j == 0:
            snapcpy("xq20", xq2_sb[0][:, :], wait=(sA, q2a[3]))
            snapcpy("s3r", bc_s3[:, :])
            snapcpy("r3r", bc_r3[:, :], wait=(sBC, gr3))

        yev = None
        for i2 in range(8):
            if i2 < 2:
                wslab = ([(sWQG, wq["w2_last"])] if j == 0
                         else [(sP, u2b[j - 1][f"mm2_{6 + i2}"])])
            else:
                wslab = [(sP, st[f"mm2_{i2 - 2}"])]
            sd = dma("scalar", sW2S[i2 % 2], w2s_p[i2 % 2][:, :], w2q_dram[i2],
                     wait=wslab)
            bk2 = 2 + (i2 % 2)
            pv = None
            for kk in range(32):
                w_ = []
                if kk == 0:
                    w_.append((sW2S[i2 % 2], sd))
                if i2 == 0 and kk % 8 == 0:
                    w_.append((sA, q2a[kk // 8]))
                pv = pe_pb(bk2, lambda t, i2=i2, kk=kk, bk2=bk2: t.matmul(
                    pb[bk2][:, 0:SC], w2s_p[i2 % 2][:, 128 * kk:128 * (kk + 1)],
                    xq2_sb[kk][:, :], start=(kk == 0), stop=(kk == 31)),
                    wait=w_ or None)
            st[f"mm2_{i2}"] = pv

            def fy(v, i2=i2, p=p, bk2=bk2):
                v.tensor_tensor(y_sb[i2][:, :], pb[bk2][:, 0:SC], bc_r3[:, :],
                                Alu.mult)
                return v.tensor_tensor(y_sb[i2][:, :], y_sb[i2][:, :],
                                       x1_p[p][i2][:, :], Alu.add)
            wy = [(sP, pv), (sBC, gr3)]
            if j >= 1:
                wy.append((sY, u2b[j - 1]["yd"][7]))
            yev = dv_pb(bk2, fy, wait=wy)
            if j == 0 and i2 == 0:
                snapcpy("y0", y_sb[0][:, :])
            st.setdefault("yd", [None] * 8)
            st["yd"][i2] = dma("sync", sY,
                               y_out[128 * i2:128 * (i2 + 1), SC * j:SC * (j + 1)],
                               y_sb[i2][:, :], wait=[(sV, yev)])
        st["mm2_done"] = st["mm2_7"]
        st["yev_last"] = yev

    tap("halo", [(xqh_l[i][:, :], BF16, (128, 1), (sV, vHM)) for i in range(8)]
         + [(xqh_r[i][:, :], BF16, (128, 1), (sV, vHM))
            for i in range(8)]
         + [(r1h[0:1, :], F32, (1, 2), (sV, vHM))])
    emit_2a(0)
    tap("outs0", [(outs_sb[k][:, :], BF16, (128, SC), (sOUT, u2a[0]["od"]))
                  for k in range(8)])
    tap("x1_0", [(x1_p[0][k][:, :], F32, (128, SC), (sV, u2a[0]["x1_done"]))
                 for k in range(8)])
    tap("xq_0", [(xq_p[0][k][:, :], BF16, (128, SC + 2), (sV, u2a[0]["xq_done"]))
                 for k in range(8)])
    tap("rows0", [(r_C[0:1, :], F32, (1, SC), (sV, u2a[0]["xq_done"])),
                  (r_D[0:1, :], F32, (1, SC), (sV, u2a[0]["xq_done"])),
                  (r_r1[0][0:1, :], F32, (1, SC + 2), (sV, u2a[0]["xq_done"]))])
    for j in range(1, NSC):
        emit_2a(j)
        emit_2b(j - 1)
        if j == 1 and _TAP["name"] in ("g0", "xq2_0", "y0"):
            tap("g0", [(g_sb[i][:, :], F32, (128, SC), (sV, u2b[0]["q2_last"]))
                       for i in range(4)])
            tap("xq2_0", [(xq2_sb[i][:, :], BF16, (128, SC), (sV, u2b[0]["q2_last"]))
                          for i in range(4)])
            tap("y0", [(y_sb[k][:, :], F32, (128, SC), (sV, u2b[0]["yev_last"]))
                       for k in range(8)])
    emit_2b(NSC - 1)

    snap_dump()
    wait("sync", sY, CNT["sY"])
    wait("gpsimd", sY, CNT["sY"])
    wait("vector", sY, CNT["sY"])

    with nc.Block() as block:
        @block.tensor
        def _(t):
            for f in STREAM["tensor"]:
                f(t)

        @block.scalar
        def _(a):
            for f in STREAM["scalar"]:
                f(a)

        @block.vector
        def _(v):
            vp = _DrainProxy(v)
            for f in STREAM["vector"]:
                f(vp)

        @block.gpsimd
        def _(g):
            for f in STREAM["gpsimd"]:
                f(g)

        @block.sync
        def _(s):
            for f in STREAM["sync"]:
                f(s)

    for g in reversed(keep):
        g.__exit__(None, None, None)
    return nc


_DVE_OPS = frozenset((
    "tensor_tensor", "tensor_scalar", "tensor_single_scalar", "tensor_copy",
    "memset", "memzero", "reduce_sum", "reduce_max", "tensor_reduce",
    "reciprocal", "scalar_tensor_tensor", "select", "transpose",
    "tensor_add", "tensor_sub", "tensor_mul", "tensor_max", "tensor_relu",
    "tensor_scalar_add", "tensor_scalar_sub", "tensor_scalar_mul",
    "tensor_scalar_max", "tensor_scalar_min", "copy_predicated",
))


class _DrainProxy:

    def __init__(self, h):
        object.__setattr__(self, "_h", h)
        object.__setattr__(self, "_dirty", set())

    @staticmethod
    def _name(a):
        t = getattr(a, "tensor", None)
        return getattr(t, "name", None) if t is not None else None

    def __getattr__(self, nm):
        h = object.__getattribute__(self, "_h")
        dirty = object.__getattribute__(self, "_dirty")
        a = getattr(h, nm)
        if nm == "drain":
            def wd(*args, **kw):
                dirty.clear()
                return a(*args, **kw)
            return wd
        if nm in _DVE_OPS:
            def w(*args, **kw):
                names = [self._name(x) for x in list(args) + list(kw.values())]
                names = [x for x in names if x]
                outn = names[0] if names else None
                if any(x in dirty for x in names[1:]):
                    h.drain()
                    dirty.clear()
                if outn:
                    dirty.add(outn)
                return a(*args, **kw)
            return w
        return a


_NC_CACHE = {}


def _get_nc():
    if "nc" not in _NC_CACHE:
        _NC_CACHE["nc"] = build_program()
    return _NC_CACHE["nc"]


def make_in_maps(L0_attn_tokens, L0_attn_norm_w, L0_attn_a, L0_attn_b, L0_attn_c,
                 L0_attn_skip, L0_attn_w_out, L0_attn_b_out,
                 L0_ffn_w1, L0_ffn_conv_w, L0_ffn_conv_b, L0_ffn_w2):
    f32 = np.float32
    X = np.ascontiguousarray(np.asarray(L0_attn_tokens, f32).reshape(NT, D))
    XT = np.ascontiguousarray(X.T)
    a = np.asarray(L0_attn_a, f32)
    b = np.asarray(L0_attn_b, f32)
    c = np.asarray(L0_attn_c, f32)
    skip = np.asarray(L0_attn_skip, f32)
    nw = np.asarray(L0_attn_norm_w, f32)
    w_out = np.asarray(L0_attn_w_out, f32)
    b_out = np.asarray(L0_attn_b_out, f32)
    cw = np.asarray(L0_ffn_conv_w, f32).reshape(HID, 3)
    cb = np.asarray(L0_ffn_conv_b, f32).reshape(HID, 1)
    w_outT = np.ascontiguousarray(w_out.T)
    w1T = np.ascontiguousarray(np.asarray(L0_ffn_w1, f32).T)
    w2T = np.ascontiguousarray(np.asarray(L0_ffn_w2, f32).T)
    ident = np.eye(128, dtype=f32)
    onesc = np.ones((128, 1), f32)

    def blkdiag(m0, m1):
        z = np.zeros((128, 128), f32)
        z[0:64, 0:64] = m0
        z[64:128, 64:128] = m1
        return z

    in_maps = []
    for cid in range(NCORE):
        h0, h1 = 2 * cid, 2 * cid + 1
        hm = np.ones((128, 2), f32)
        if cid in (0, 4):
            hm[:, 0] = 0.0
        if cid in (3, 7):
            hm[:, 1] = 0.0
        rows = slice(RT * cid, RT * (cid + 1))
        in_maps.append({
            "x_tok": np.ascontiguousarray(X[rows]),
            "x_feat": np.ascontiguousarray(XT[128 * cid:128 * (cid + 1), :]),
            "x_ftok": np.ascontiguousarray(XT[:, rows]),
            "x_edge": np.ascontiguousarray(XT[:, [RT * cid, RT * (cid + 1) - 1]]),
            "a_blk": blkdiag(a[h0], a[h1]),
            "b_blk": blkdiag(b[h0], b[h1]),
            "c_blk": blkdiag(c[h0], c[h1]),
            "skip2": np.ascontiguousarray(np.concatenate([skip[h0], skip[h1]])[:, None]),
            "nw_sl": np.ascontiguousarray(nw[128 * cid:128 * (cid + 1)][:, None]),
            "w_outT": w_outT,
            "b_out_c": np.ascontiguousarray(b_out[:, None]),
            "w1T_in": w1T,
            "w2T_in": w2T,
            "cw_in": cw,
            "cb_in": cb,
            "ident_in": ident,
            "onesc_in": onesc,
            "hmask_in": hm,
            "mgc_in": np.concatenate([np.full((128, 1), MAGIC, f32),
                                      np.full((128, 1), -MAGIC, f32)], axis=1),
        })

    return in_maps


def kernel(**inputs):
    in_maps = make_in_maps(**inputs)
    nc = _get_nc()
    with allow_ici_all_to_all():
        res = run_bass_kernel_spmd(nc, in_maps, core_ids=list(range(NCORE)))
    Y = np.empty((D, NT), np.float32)
    for cid in range(NCORE):
        Y[:, RT * cid:RT * (cid + 1)] = res.results[cid]["y_out"]
    return np.ascontiguousarray(Y.T).reshape(2, T, D)
